# version 18 (final)
# speedup vs baseline: 1.1331x; 1.0038x over previous
import contextlib
import numpy as np
import concourse.bass as bass
import concourse.mybir as mybir
from concourse.bass_utils import run_bass_kernel_spmd

F32 = mybir.dt.float32
BF16 = mybir.dt.bfloat16
I32 = mybir.dt.int32
I8 = mybir.dt.int8
AF = mybir.ActivationFunctionType
ALU = mybir.AluOpType
AX = mybir.AxisListType

D = 1024
NOPOOL = set()
KT = D // 128
NCORES = 8
ENGINES = ("tensor", "vector", "scalar", "gpsimd", "sync")
STREAMS = ENGINES


class ES:

    def __init__(self, name, dma=False):
        self.name = name
        self.step = 16 if dma else 1
        self.count = 0
        self.ops = []
        self._waited = {}

    def raw(self, f):
        self.ops.append(("op", f))

    def done(self, f):
        self.count += self.step
        self.ops.append(("inc", f))
        return (self.name, self.count)

    def tok(self):
        return (self.name, self.count)

    def wait(self, token):
        if token is None:
            return
        src, val = token
        if val == -1:
            if self._waited.get(src, 0) != -1:
                self._waited[src] = -1
                self.ops.append(("waitfull", src))
            return
        if val <= 0 or (src == self.name and val >= self.count + self.step):
            assert not (src == self.name and val > self.count), "self-wait on future token"
        if val <= self._waited.get(src, 0) and self._waited.get(src, 0) != -1:
            return
        if self._waited.get(src, 0) != -1:
            self._waited[src] = val
        self.ops.append(("wait", src, val))

    def wait_all(self, tokens):
        for t in tokens:
            self.wait(t)


class PsumPool:

    def __init__(self, banks):
        self.banks = banks
        self.last_read = [None] * len(banks)
        self.rr = 0

    def acquire(self):
        i = self.rr
        self.rr = (self.rr + 1) % len(self.banks)
        return i, self.banks[i], self.last_read[i]

    def acquire_n(self, n):
        out = [self.acquire() for _ in range(n)]
        return out

    def release(self, i, consumer_token):
        self.last_read[i] = consumer_token


class Ctx:
    def __init__(self, nc, T, TC, SUB=128):
        self.nc = nc
        self.T = T
        self.TC = TC
        self.SUB = SUB
        self.NCH = T // TC
        self.es = contextlib.ExitStack()
        self.streams = {s: ES(s) for s in STREAMS}
        self.sb_bytes = 0
        self._name_i = 0

    def channel(self, base):
        name = self.name("ch_" + base)
        ch = ES(name, dma=True)
        self.streams[name] = ch
        return ch

    @property
    def Te(self): return self.streams["tensor"]
    @property
    def V(self): return self.streams["vector"]
    @property
    def S(self): return self.streams["scalar"]
    @property
    def G(self): return self.streams["gpsimd"]
    @property
    def Y(self): return self.streams["sync"]
    def name(self, base):
        self._name_i += 1
        return f"{base}_{self._name_i}"

    def sbuf(self, base, shape, dtype):
        per_part = int(np.prod(shape[1:])) if len(shape) > 1 else 1
        nbytes = per_part * (2 if dtype == BF16 else 4)
        self.sb_bytes += 128 * nbytes
        if getattr(self, "_scr_on", False) and base not in NOPOOL:
            assert len(shape) == 2, f"scratch tensors must be 2D: {base} {shape}"
            return self.scr(dtype, shape[0], shape[1], base=0)
        return self.es.enter_context(self.nc.sbuf_tensor(self.name(base), list(shape), dtype))

    def scratch_init(self, pb_elems, pf_elems):
        self._poolb = self.es.enter_context(
            self.nc.sbuf_tensor(self.name("poolb"), [128, pb_elems], BF16))
        self._poolf = self.es.enter_context(
            self.nc.sbuf_tensor(self.name("poolf"), [128, pf_elems], F32))
        self._pool_sz = {BF16: pb_elems, F32: pf_elems}
        self._scr_on = False

    def scratch_begin(self):
        self._pool_off = {BF16: 0, F32: 0}
        self._shares = {}
        self._scr_on = True

    def scratch_end(self):
        self._scr_on = False

    def scr(self, dtype, nparts, elems, base=0, share=None, share_elems=None, share_off=0):
        pool = self._poolb if dtype == BF16 else self._poolf
        if share is not None:
            if share not in self._shares:
                off = (self._pool_off[dtype] + 15) // 16 * 16
                self._pool_off[dtype] = off + share_elems
                self._shares[share] = off
            off = self._shares[share] + share_off
        else:
            off = (self._pool_off[dtype] + 15) // 16 * 16
            self._pool_off[dtype] = off + elems
        assert self._pool_off[dtype] <= self._pool_sz[dtype], \
            f"pool overflow dtype={dtype} need {self._pool_off[dtype]} > {self._pool_sz[dtype]}"
        return pool[base:base + nparts, off:off + elems]

    def psum(self, base, shape, dtype=F32):
        return self.es.enter_context(self.nc.psum_tensor(self.name(base), list(shape), dtype))

    def hdma(self, ch, out, in_, wait=()):
        self.Y.wait_all(wait)
        ch.count += 16
        val = ch.count
        self.Y.ops.append(("dma", lambda eng, o=out, i=in_: eng.dma_start(out=o, in_=i), ch.name))
        return (ch.name, val)

    def gdma(self, ch, out, in_, wait=(), slow=False):
        self.G.wait_all(wait)
        ch.count += 16
        val = ch.count
        self.G.ops.append(("dma", lambda eng, o=out, i=in_, s=slow:
                           eng.dma_start(out=o, in_=i, allow_slow_non_contiguous=s), ch.name))
        return (ch.name, val)

    def barrier(self):
        toks = [s.tok() for s in self.streams.values()]
        for e in ENGINES:
            self.streams[e].wait_all(toks)

    def build(self, block):
        sems = {s: self.es.enter_context(self.nc.semaphore(self.name("sem_" + s)))
                for s in self.streams}

        def runner(stream):
            def _(eng):
                for op in stream.ops:
                    if op[0] == "op":
                        op[1](eng)
                    elif op[0] == "inc":
                        ins = op[1](eng)
                        ins.then_inc(sems[stream.name], stream.step)
                    elif op[0] == "wait":
                        eng.wait_ge(sems[op[1]], op[2])
                    elif op[0] == "waitfull":
                        eng.wait_ge(sems[op[1]], self.streams[op[1]].count)
                    elif op[0] == "dma":
                        op[1](eng).then_inc(sems[op[2]], 16)
            return _
        block.tensor(runner(self.streams["tensor"]))
        block.vector(runner(self.streams["vector"]))
        block.scalar(runner(self.streams["scalar"]))
        block.gpsimd(runner(self.streams["gpsimd"]))
        block.sync(runner(self.streams["sync"]))



class Common:
    def __init__(self, ctx):
        self.setup_ch = ctx.channel("setup")
        self.setup_tok = (self.setup_ch.name, -1)
        self.setupg_ch = ctx.channel("setupg")
        self.setupg_tok = (self.setupg_ch.name, -1)
        self.idf = ctx.sbuf("identf", [128, 128], F32)
        t = ctx.G.done(lambda eng, s=self.idf: eng.memset(s[:], 0.0))
        ctx.G.wait(t)
        self.idf_tok = ctx.G.done(lambda eng, s=self.idf: eng.affine_select(
            out=s[:], in_=s[:], compare_op=ALU.not_equal, fill=1.0,
            base=0, pattern=[[-1, 128]], channel_multiplier=1))
        self.idb = ctx.sbuf("identb", [128, 128], BF16)
        tb = ctx.G.done(lambda eng, s=self.idb: eng.memset(s[:], 0.0))
        ctx.G.wait(tb)
        self.idb_tok = ctx.G.done(lambda eng, s=self.idb: eng.affine_select(
            out=s[:], in_=s[:], compare_op=ALU.not_equal, fill=1.0,
            base=0, pattern=[[-1, 128]], channel_multiplier=1))
        self.onescol = ctx.sbuf("onescol", [128, 1], F32)
        self.onesrow = ctx.sbuf("onesrow", [1, 512], F32)
        self.ones16 = ctx.sbuf("ones16", [16, 1], BF16)
        self.onesrowb = ctx.sbuf("onesrowb", [1, 512], BF16)
        t1 = ctx.V.done(lambda eng, s=self.onescol: eng.memset(s[:], 1.0))
        t2 = ctx.V.done(lambda eng, s=self.onesrow: eng.memset(s[:], 1.0))
        t3 = ctx.V.done(lambda eng, s=self.ones16: eng.memset(s[:], 1.0))
        t4 = ctx.V.done(lambda eng, s=self.onesrowb: eng.memset(s[:], 1.0))
        self.ones_tok = t4
        self.ctx = ctx
        self._consts = {}
        self._pending_setup = True

    def c(self, val, p=128):
        key = float(val)
        if key not in self._consts:
            assert self._pending_setup, "register consts before finish_setup"
            cell = self.ctx.sbuf("cc", [128, 1], F32)
            self.ctx.V.done(lambda eng, cell=cell, v=key: eng.memset(cell[:], v))
            self._consts[key] = cell
        return self._consts[key][0:p, 0:1]

    def crecip(self, denom):
        key = ("recip", float(denom))
        if key not in self._consts:
            assert self._pending_setup
            cell = self.ctx.sbuf("cr", [128, 1], F32)
            t = self.ctx.V.done(lambda eng, cell=cell, v=float(denom): eng.memset(cell[:], v))
            self.ctx.V.wait(t)
            self.ctx.V.done(lambda eng, cell=cell: eng.reciprocal(cell[:], cell[:]))
            self._consts[key] = cell
        return self._consts[key]

    def finish_setup(self):
        self._pending_setup = False
        self.ctx.barrier()


def load_bias_cols(ctx, pp, com, bdram, n, name="b", shaped=False):
    kt = n // 128
    if not hasattr(com, "browbuf"):
        com.browbuf = ctx.sbuf("browbuf", [16, 128], F32)
        com.brow_war = None
        com.bias_ch = ctx.channel("biasch")
    row = com.browbuf[0:kt, :]
    t_ld = ctx.hdma(com.bias_ch, row,
                    bdram if shaped else bdram.rearrange("(a b) -> a b", b=128),
                    wait=[com.brow_war])
    bi, ps, war = pp.acquire()
    ctx.Te.wait_all([t_ld, war, com.idf_tok])
    t_tr = ctx.Te.done(lambda eng, row=row, ps=ps, com=com, kt=kt:
                       eng.transpose(ps[0:128, 0:kt], row, com.idf[0:kt, 0:kt]))
    com.brow_war = t_tr
    out = ctx.sbuf(name, [128, kt], F32)
    ctx.V.wait(t_tr)
    t_ev = ctx.V.done(lambda eng, out=out, ps=ps, kt=kt: eng.tensor_copy(out[:], ps[0:128, 0:kt]))
    pp.release(bi, t_ev)
    return out, t_ev


def load_bias_row(ctx, com, bdram, n, name="brow", dtype=F32):
    out = ctx.sbuf(name, [1, n], dtype)
    src_ap = bdram.rearrange("(a b) -> a b", b=n) if bdram.ndim == 1 else bdram
    if dtype == F32:
        ctx.hdma(com.setup_ch, out[:], src_ap)
        return out, com.setup_tok
    ctx.gdma(com.setupg_ch, out[:], src_ap)
    return out, com.setupg_tok



class WResident:

    def __init__(self, ctx, wdram, din, dout, name="w", sbuf=None, ch=None, war=None):
        self.kt = din // 128
        self.dout = dout
        self.sb = sbuf if sbuf is not None else ctx.sbuf(name, [128, self.kt * dout], BF16)
        self.ch = ch if ch is not None else ctx.channel(name)
        if self.kt == 1:
            self.tok = ctx.gdma(self.ch, self.sb[:, 0:dout], wdram[:, :],
                                wait=[war] if war is not None else ())
        else:
            self.tok = ctx.gdma(
                self.ch,
                self.sb[:, 0:self.kt * dout].rearrange("p (a n) -> p a n", a=self.kt),
                wdram.rearrange("(a p) n -> p a n", p=128),
                wait=[war] if war is not None else ())

    def view(self, ki, mo, mw=128):
        return self.sb[:, ki * self.dout + mo: ki * self.dout + mo + mw]



class KRowStream:

    def __init__(self, ctx, name, dout, krows=128):
        self.ctx = ctx
        self.dout = dout
        self.krows = krows
        self.slots = [ctx.sbuf(name + "s", [krows, dout], BF16) for _ in range(2)]
        self.chans = [ctx.channel(name + "0"), ctx.channel(name + "1")]
        self.war = [None, None]

    def load(self, wdram, ki):
        s = ki % 2
        tok = self.ctx.gdma(self.chans[s], self.slots[s][:],
                            wdram[ki * self.krows:(ki + 1) * self.krows, :],
                            wait=[self.war[s]])
        return self.slots[s], tok

    def used(self, ki, mm_tok):
        self.war[ki % 2] = mm_tok


def emit_linear_krows(ctx, pp, stream, wdram, din, dout, rhs_aps, rhs_tok, evac,
                      krows=128, bias_rank1=None, extra_k=None):
    kt = din // krows
    mt = dout // 128
    assert mt <= len(pp.banks)
    banks = pp.acquire_n(mt)
    ctx.Te.wait_all([w for (_, _, w) in banks])
    ctx.Te.wait(rhs_tok)
    n = rhs_aps[0].shape[-1]
    n_tail = (1 if bias_rank1 is not None else 0) + (len(extra_k) if extra_k else 0)
    mm_tok = None
    for ki in range(kt):
        slot, ltok = stream.load(wdram, ki)
        ctx.Te.wait(ltok)
        for mi in range(mt):
            _, ps, _ = banks[mi]
            stop = (ki == kt - 1) and n_tail == 0
            f = (lambda eng, ps=ps, slot=slot, mi=mi, rhs=rhs_aps[ki], ki=ki, stop=stop, n=n:
                 eng.matmul(ps[:, 0:n], slot[:, mi * 128:(mi + 1) * 128], rhs,
                            start=(ki == 0), stop=stop))
            if mi == mt - 1:
                mm_tok = ctx.Te.done(f)
                stream.used(ki, mm_tok)
            else:
                ctx.Te.raw(f)
    tails = []
    if bias_rank1 is not None:
        lf, rrow, btok = bias_rank1
        tails.append((lf, rrow, btok))
    if extra_k:
        tails.extend(extra_k)
    for j, (lf, r_ap, tok2) in enumerate(tails):
        ctx.Te.wait(tok2)
        for mi in range(mt):
            _, ps, _ = banks[mi]
            stop = (j == len(tails) - 1)
            f = (lambda eng, ps=ps, l=lf(mi), r=r_ap, n=n, stop=stop:
                 eng.matmul(ps[:, 0:n], l, r[0:1, 0:n], start=False, stop=stop))
            mm_tok = ctx.Te.done(f) if (stop and mi == mt - 1) else ctx.Te.raw(f)
    ev_tok = evac([(banks[mi][0], banks[mi][1], mi) for mi in range(mt)], mm_tok)
    for mi in range(mt):
        pp.release(banks[mi][0], ev_tok)
    return ev_tok


def emit_linear_resident(ctx, pp, w, rhs_aps, rhs_tok, evac, n=None, extra_k=None):
    kt = w.kt
    mt = max(1, w.dout // 128)
    if n is None:
        n = rhs_aps[0].shape[-1]
    ctx.Te.wait_all([w.tok, rhs_tok])
    ntail = len(extra_k) if extra_k else 0
    last = None
    for mi in range(mt):
        mw = min(128, w.dout - mi * 128)
        bi, ps, war = pp.acquire()
        ctx.Te.wait(war)
        for ki in range(kt):
            stop = (ki == kt - 1) and ntail == 0
            f = (lambda eng, ps=ps, w=w, ki=ki, mi=mi, mw=mw, rhs=rhs_aps[ki], n=n, stop=stop:
                 eng.matmul(ps[0:mw, 0:n], w.view(ki, mi * 128, mw), rhs,
                            start=(ki == 0), stop=stop))
            mm_tok = ctx.Te.done(f) if (ki == kt - 1 and ntail == 0) else ctx.Te.raw(f)
        if extra_k:
            for j, (lf, r_ap, tok2) in enumerate(extra_k):
                ctx.Te.wait(tok2)
                stop = (j == ntail - 1)
                f = (lambda eng, ps=ps, l=lf(mi), r=r_ap, n=n, mw=mw, stop=stop:
                     eng.matmul(ps[0:mw, 0:n], l[:, 0:mw], r[0:1, 0:n], start=False, stop=stop))
                mm_tok = ctx.Te.done(f) if stop else ctx.Te.raw(f)
        ev = evac(bi, ps, mi, mw, mm_tok)
        pp.release(bi, ev)
        last = ev
    return last



class LNState:
    def __init__(self, ctx):
        self.sq = ctx.sbuf("lnsq", [128, 2 * ctx.TC], F32)
        self.sq_war = None
        self.rows = ctx.sbuf("lnrows", [1, 5 * ctx.TC], F32)
        self.t1 = ctx.sbuf("lnt1", [128, ctx.TC], F32)
        self.t1_war = None

    def row(self, j, ctx):
        return self.rows[0:1, j * ctx.TC:(j + 1) * ctx.TC]


def emit_ln_chunk(ctx, pp, com, ln, R, r_tok, c, xn, xn_war=None):
    TC = ctx.TC
    cs = slice(c * TC, (c + 1) * TC)
    bi_s, ps_s, war_s = pp.acquire()
    bi_q, ps_q, war_q = pp.acquire()
    ctx.Te.wait_all([war_s, war_q, r_tok])
    ctx.S.wait(r_tok)
    t_mm = None
    for half in range(4):
        ctx.S.wait(ln.sq_war)
        for i4 in range(2):
            i = half * 2 + i4
            f = (lambda eng, ln=ln, R=R, i=i, i4=i4, cs=cs, TC=TC:
                 eng.activation(ln.sq[:, i4 * TC:(i4 + 1) * TC], R[i][:, cs], AF.Square))
            t_sq = ctx.S.done(f) if i4 == 1 else ctx.S.raw(f)
        ctx.Te.wait(t_sq)
        for i4 in range(2):
            i = half * 2 + i4
            ctx.Te.raw(lambda eng, ps=ps_s, com=com, R=R, i=i, cs=cs:
                       eng.matmul(ps[0:1, :], com.onescol[:], R[i][:, cs],
                                  start=(i == 0), stop=(i == KT - 1)))
            f = (lambda eng, ps=ps_q, com=com, ln=ln, i4=i4, i=i, TC=TC:
                 eng.matmul(ps[0:1, :], com.onescol[:], ln.sq[:, i4 * TC:(i4 + 1) * TC],
                            start=(i == 0), stop=(i == KT - 1)))
            t_mm = ctx.Te.done(f) if i4 == 1 else ctx.Te.raw(f)
        ln.sq_war = t_mm
    mean, var, rstd, mr = (ln.row(j, ctx) for j in range(4))
    tmp = ln.row(4, ctx)
    ctx.V.wait(t_mm)
    ctx.V.raw(lambda eng, mean=mean, ps=ps_s: eng.tensor_scalar_mul(mean, ps[0:1, :], 1.0 / D))
    t_v0 = ctx.V.done(lambda eng, var=var, ps=ps_q: eng.tensor_scalar_mul(var, ps[0:1, :], 1.0 / D))
    ctx.V.wait(t_v0)
    t_tm = ctx.V.done(lambda eng, tmp=tmp, mean=mean: eng.tensor_mul(tmp, mean, mean))
    ctx.V.wait(t_tm)
    t_v1 = ctx.V.done(lambda eng, var=var, tmp=tmp: eng.tensor_sub(var, var, tmp))
    ctx.S.wait(t_v1)
    _eps = com.c(1e-5, p=1)
    _nh = com.c(-0.5, p=1)
    t_s1 = ctx.S.done(lambda eng, tmp=tmp, var=var, b=_eps: eng.activation(tmp, var, AF.Ln, bias=b))
    ctx.S.wait(t_s1)
    t_s2 = ctx.S.done(lambda eng, rstd=rstd, tmp=tmp, s=_nh: eng.activation(rstd, tmp, AF.Exp, scale=s))
    ctx.V.wait(t_s2)
    t_v2 = ctx.V.done(lambda eng, mr=mr, mean=mean, rstd=rstd: eng.tensor_mul(mr, mean, rstd))
    pp.release(bi_s, t_v0)
    pp.release(bi_q, t_v0)
    bi_r, ps_r, war_r = pp.acquire()
    bi_m, ps_m, war_m = pp.acquire()
    ctx.Te.wait_all([war_r, war_m, t_v2])
    t_br = ctx.Te.done(lambda eng, ps=ps_r, com=com, rstd=rstd:
                       eng.matmul(ps[:, :], com.onesrow[0:1, 0:128], rstd, start=True, stop=True))
    t_bm = ctx.Te.done(lambda eng, ps=ps_m, com=com, mr=mr:
                       eng.matmul(ps[:, :], com.onesrow[0:1, 0:128], mr, start=True, stop=True))
    ctx.V.wait_all([t_br, t_bm, xn_war])
    last = None
    for i in range(KT):
        ctx.V.wait(ln.t1_war)
        t_m = ctx.V.done(lambda eng, ln=ln, R=R, i=i, cs=cs, ps=ps_r:
                         eng.tensor_mul(ln.t1[:], R[i][:, cs], ps[:, :]))
        ctx.V.wait(t_m)
        last = ctx.V.done(lambda eng, ln=ln, xn=xn, i=i, ps=ps_m:
                          eng.tensor_sub(xn[i], ln.t1[:], ps[:, :]))
        ln.t1_war = last
    pp.release(bi_r, last)
    pp.release(bi_m, last)
    return last



class MGroupStream:

    def __init__(self, ctx, name, din, mgw, nslots=2):
        self.ctx = ctx
        self.kt = din // 128
        self.mgw = mgw
        self.nslots = nslots
        self.s1 = [ctx.sbuf(name + "a", [128, self.kt * mgw], BF16) for _ in range(nslots)]
        self.s2 = [ctx.sbuf(name + "b", [128, self.kt * mgw], BF16) for _ in range(nslots)]
        self.chans = [ctx.channel(name + str(p)) for p in range(nslots)]
        self.war = [None] * nslots

    def load(self, w1d, w2d, mg):
        p = mg % self.nslots
        tok = None
        for sb, wd in ((self.s1[p], w1d), (self.s2[p], w2d)):
            tok = self.ctx.gdma(
                self.chans[p],
                sb[:, 0:self.kt * self.mgw].rearrange("p (a n) -> p a n", a=self.kt),
                wd[:, mg * self.mgw:(mg + 1) * self.mgw].rearrange(
                    "(a p) n -> p a n", p=128),
                wait=[self.war[p]] if tok is None else ())
        return tok

    def view(self, which, ki, msub, mg=0):
        sb = (self.s1 if which == 0 else self.s2)[mg % self.nslots]
        return sb[:, ki * self.mgw + msub * 128: ki * self.mgw + msub * 128 + 128]


class SwigluBufs:
    def __init__(self, ctx, H, mgw, w3s=None, nslots=2):
        self.H = H
        self.mgw = mgw
        self.hbuf = ctx.sbuf("hbuf", [128, (H // 128) * ctx.TC], BF16)
        self.gt = [ctx.sbuf("gt", [128, ctx.TC], BF16)]
        self.gt_war = [None]
        self.mgs = MGroupStream(ctx, "mgs", D, mgw, nslots=nslots)
        self.w3s = w3s if w3s is not None else KRowStream(ctx, "w3s", D)


def emit_swiglu_chunk(ctx, pp, com, sw, xn, xn_tok, w1d, w2d, w3d, s1c, s3c, R, c):
    TC = ctx.TC
    cs = slice(c * TC, (c + 1) * TC)
    H = sw.H
    nmg = H // sw.mgw
    msubs = sw.mgw // 128
    h_tok = None
    pre = sw.mgs.nslots > 1
    if pre:
        wtoks = [sw.mgs.load(w1d, w2d, 0)]
    for mg in range(nmg):
        if pre:
            if mg + 1 < nmg:
                wtoks.append(sw.mgs.load(w1d, w2d, mg + 1))
            wtok = wtoks[mg]
        else:
            wtok = sw.mgs.load(w1d, w2d, mg)
        for msub in range(msubs):
            kh = mg * msubs + msub
            bi_g, ps_g, war_g = pp.acquire()
            bi_u, ps_u, war_u = pp.acquire()
            ctx.Te.wait_all([war_g, war_u, wtok, xn_tok])
            for ki in range(KT):
                ctx.Te.raw(lambda eng, ps=ps_g, sw=sw, ki=ki, msub=msub, xn=xn, mg=mg:
                           eng.matmul(ps[:, :], sw.mgs.view(0, ki, msub, mg), xn[ki],
                                      start=(ki == 0), stop=(ki == KT - 1)))
                f = (lambda eng, ps=ps_u, sw=sw, ki=ki, msub=msub, xn=xn, mg=mg:
                     eng.matmul(ps[:, :], sw.mgs.view(1, ki, msub, mg), xn[ki],
                                start=(ki == 0), stop=(ki == KT - 1)))
                t_mm = ctx.Te.done(f) if ki == KT - 1 else ctx.Te.raw(f)
            gsel = kh % len(sw.gt)
            gt = sw.gt[gsel]
            ctx.S.wait_all([t_mm, sw.gt_war[gsel]])
            if s1c is None:
                t_g = ctx.S.done(lambda eng, gt=gt, ps=ps_g:
                                 eng.activation(gt[:], ps[:, :], AF.Silu))
            else:
                t_g = ctx.S.done(lambda eng, gt=gt, ps=ps_g, s=s1c[0:128, 0:1]:
                                 eng.activation(gt[:], ps[:, :], AF.Silu, scale=s))
            ctx.V.wait_all([t_g, t_mm])
            s1arg = 1.0 if s1c is None else s1c[0:128, 0:1]
            h_tok = ctx.V.done(lambda eng, sw=sw, kh=kh, ps=ps_u, s1=s1arg, gt=gt, TC=TC:
                               eng.scalar_tensor_tensor(
                                   sw.hbuf[:, kh * TC:(kh + 1) * TC], ps[:, :], s1, gt[:],
                                   ALU.mult, ALU.mult))
            sw.gt_war[gsel] = h_tok
            pp.release(bi_g, t_g)
            pp.release(bi_u, h_tok)
            sw.mgs.war[mg % sw.mgs.nslots] = t_mm
    rhs = [sw.hbuf[:, kh * TC:(kh + 1) * TC] for kh in range(H // 128)]

    s3arg = 1.0 if s3c is None else s3c[0:128, 0:1]

    def evac(bank_list, mm_tok):
        ctx.V.wait(mm_tok)
        last = None
        for (bi, ps, mi) in bank_list:
            last = ctx.V.done(lambda eng, ps=ps, mi=mi, R=R, cs=cs, s3=s3arg:
                              eng.scalar_tensor_tensor(R[mi][:, cs], ps[:, :], s3,
                                                       R[mi][:, cs], ALU.mult, ALU.add))
        return last

    return emit_linear_krows(ctx, pp, sw.w3s, w3d, H, D, rhs, h_tok, evac)



def build_l0f(T=2048, TC=512, H=3072, mgw=256):
    nc = bass.Bass("TRN2", target_bir_lowering=False, debug=False)
    xin = nc.dram_tensor("xT", [D, T], F32, kind="ExternalInput").ap()
    wg = nc.dram_tensor("wgT", [D, H], I32, kind="ExternalInput").ap()
    wu = nc.dram_tensor("wuT", [D, H], I32, kind="ExternalInput").ap()
    wd = nc.dram_tensor("wdT", [H, D], I32, kind="ExternalInput").ap()
    xout = nc.dram_tensor("yT", [D, T], F32, kind="ExternalOutput").ap()
    ctx = Ctx(nc, T, TC)
    with ctx.es:
        banks = [ctx.psum("bank", [128, TC]) for _ in range(8)]
        pp = PsumPool(banks)
        com = Common(ctx)
        com.c(1e-5); com.c(-0.5)
        r127 = com.crecip(127.0)
        com.finish_setup()
        ln = LNState(ctx)
        sw = SwigluBufs(ctx, H, mgw)
        R = [ctx.sbuf("R", [128, T], F32) for _ in range(KT)]
        XN = ctx.sbuf("XN", [128, KT * TC], BF16)
        xn = [XN[:, i * TC:(i + 1) * TC] for i in range(KT)]
        ch_r = ctx.channel("rload")
        r_toks = [ctx.hdma(ch_r, R[i][:], xin[i * 128:(i + 1) * 128, :]) for i in range(KT)]
        r_tok = r_toks[-1]
        for c in range(ctx.NCH):
            t_ln = emit_ln_chunk(ctx, pp, com, ln, R, r_tok, c, xn)
            t_f = emit_swiglu_chunk(ctx, pp, com, sw, xn, t_ln, wg, wu, wd,
                                    r127, r127, R, c)
            r_tok = t_f
        ctx.barrier()
        ch_o = ctx.channel("store")
        st = [ctx.hdma(ch_o, xout[i * 128:(i + 1) * 128, :], R[i][:]) for i in range(KT)]
        ctx.Y.wait(st[-1])
        with nc.Block() as block:
            ctx.build(block)
    return nc



class SSMBufs:
    def __init__(self, ctx, pb=None):
        TC = ctx.TC
        self.DT = ctx.sbuf("DT", [128, KT * TC], F32)
        self.U = ctx.sbuf("U", [128, KT * TC], BF16)
        self.Z = ctx.sbuf("Z", [128, KT * TC], BF16)
        self.YZ = ctx.sbuf("YZ", [128, KT * TC], BF16)
        self.BBH = ctx.sbuf("BBH", [128, 8 * TC], BF16)
        self.CBH = ctx.sbuf("CBH", [128, 8 * TC], BF16)
        self.HJ = ctx.sbuf("HJ", [128, 8 * TC], BF16)
        self.PB = pb if pb is not None else ctx.sbuf("PB", [128, 8 * TC], BF16)
        self.Brow = ctx.sbuf("Brow", [16, TC], BF16)
        self.Crow = ctx.sbuf("Crow", [16, TC], BF16)
        self.Brow0 = ctx.sbuf("Brow0", [1, 4 * TC], BF16)
        self.Crow0 = ctx.sbuf("Crow0", [1, 4 * TC], BF16)
        self.carry = ctx.sbuf("carry", [128, 128], F32)
        self.ab = [ctx.sbuf("ab", [128, TC], BF16) for _ in range(4)]
        self.bb = [ctx.sbuf("bb", [128, TC], BF16) for _ in range(4)]
        self.ab_war = [None] * 4
        self.bb_war = [None] * 4
        self.escr = [ctx.sbuf("escr", [128, TC], F32)]
        self.escr_war = [None]
        self.YH = ctx.sbuf("YH", [128, KT * TC], BF16)
        self.yh = [ctx.sbuf("yh", [128, TC], F32) for _ in range(2)]
        self.yh_war = [None, None]
        self.ch_bc = [ctx.channel("bcrow0"), ctx.channel("bcrow1")]
        self.bc_war = [None, None]
        self.carry_tok = None


def emit_l1a_chunk(ctx, pp, com, ln, sb, ws, xn, xn_tok, R, c,
                   wz, wx, wdt, wout, wB, wC, bz, bx, bdt, bB16, bC16, boutrow,
                   Asb, A_tok, first):
    TC = ctx.TC
    cs = slice(c * TC, (c + 1) * TC)

    def evac_dt(bank_list, mm_tok):
        ctx.S.wait(mm_tok)
        last = None
        for (bi, ps, mi) in bank_list:
            es_ = mi % len(sb.escr)
            ctx.S.wait(sb.escr_war[es_])
            t_e = ctx.S.done(lambda eng, sb=sb, es_=es_, ps=ps, b=bdt, mi=mi:
                             eng.activation(sb.escr[es_][:], ps[:, :], AF.Exp, bias=b[:, mi:mi + 1]))
            ctx.S.wait(t_e)
            last = ctx.S.done(lambda eng, sb=sb, es_=es_, mi=mi, one=com.c(1.0), TC=TC:
                              eng.activation(sb.DT[:, mi * TC:(mi + 1) * TC],
                                             sb.escr[es_][:], AF.Ln, bias=one))
            sb.escr_war[es_] = last
        return last
    t_dt = emit_linear_krows(ctx, pp, ws, wdt, D, D, xn, xn_tok, evac_dt)

    def evac_u(bank_list, mm_tok):
        ctx.S.wait(mm_tok)
        last = None
        for (bi, ps, mi) in bank_list:
            es_ = mi % len(sb.escr)
            ctx.S.wait(sb.escr_war[es_])
            t_x = ctx.S.done(lambda eng, sb=sb, es_=es_, ps=ps, b=bx, mi=mi:
                             eng.activation(sb.escr[es_][:], ps[:, :], AF.Identity, bias=b[:, mi:mi + 1]))
            ctx.V.wait_all([t_x, t_dt])
            last = ctx.V.done(lambda eng, sb=sb, es_=es_, mi=mi, TC=TC:
                              eng.tensor_mul(sb.U[:, mi * TC:(mi + 1) * TC],
                                             sb.escr[es_][:], sb.DT[:, mi * TC:(mi + 1) * TC]))
            sb.escr_war[es_] = last
        return last
    t_u = emit_linear_krows(ctx, pp, ws, wx, D, D, xn, xn_tok, evac_u)

    def evac_z(bank_list, mm_tok):
        ctx.S.wait(mm_tok)
        last = None
        for (bi, ps, mi) in bank_list:
            last = ctx.S.done(lambda eng, sb=sb, ps=ps, b=bz, mi=mi, TC=TC:
                              eng.activation(sb.Z[:, mi * TC:(mi + 1) * TC], ps[:, :],
                                             AF.Sigmoid, bias=b[:, mi:mi + 1]))
        return last
    t_z = emit_linear_krows(ctx, pp, ws, wz, D, D, xn, xn_tok, evac_z)

    def evac_b(bi, ps, mi, mw, mm_tok):
        ctx.S.wait(mm_tok)
        return ctx.S.done(lambda eng, sb=sb, ps=ps, b=bB16:
                          eng.activation(sb.Brow[:], ps[0:16, :], AF.Identity, bias=b[0:16, 0:1]))
    t_B = emit_linear_resident(ctx, pp, wB, xn, xn_tok, evac_b)

    def evac_c(bi, ps, mi, mw, mm_tok):
        ctx.S.wait(mm_tok)
        return ctx.S.done(lambda eng, sb=sb, ps=ps, b=bC16:
                          eng.activation(sb.Crow[:], ps[0:16, :], AF.Identity, bias=b[0:16, 0:1]))
    t_C = emit_linear_resident(ctx, pp, wC, xn, xn_tok, evac_c)

    carry_new = None
    yz_tok = None
    half_guard = None
    hj_war = None
    pb_war = None
    t_bc0 = [None, None]

    def collapse(qg):
        qp = qg % 2
        r0 = qg * 2
        ctx.gdma(sb.ch_bc[qp], sb.Brow0[0:1, qp * 2 * TC:(qp + 1) * 2 * TC],
                 sb.Brow[r0:r0 + 2, :], wait=[t_B, t_C, sb.bc_war[qp]])
        t_bc0[qp] = ctx.gdma(sb.ch_bc[qp], sb.Crow0[0:1, qp * 2 * TC:(qp + 1) * 2 * TC],
                             sb.Crow[r0:r0 + 2, :])
    collapse(0)
    for jh in range(2):
        t_bb = t_cc = None
        for j8 in range(8):
            qg = jh * 4 + j8 // 2
            qp = qg % 2
            if j8 % 2 == 0 and qg + 1 < 8:
                collapse(qg + 1)
            jq = j8 % 2
            for (dst, src0) in ((sb.BBH, sb.Brow0), (sb.CBH, sb.Crow0)):
                bi, ps, war = pp.acquire()
                ctx.Te.wait_all([war, t_bc0[qp], com.ones_tok])
                t_mm = ctx.Te.done(lambda eng, ps=ps, com=com, src0=src0, jq=jq, qp=qp, TC=TC:
                                   eng.matmul(ps[:, :], com.onesrowb[0:1, 0:128],
                                              src0[0:1, (qp * 2 + jq) * TC:(qp * 2 + jq + 1) * TC],
                                              start=True, stop=True))
                ctx.S.wait_all([t_mm, half_guard])
                t_ev = ctx.S.done(lambda eng, dst=dst, j8=j8, ps=ps, TC=TC:
                                  eng.activation(dst[:, j8 * TC:(j8 + 1) * TC], ps[:, :], AF.Copy))
                pp.release(bi, t_ev)
                if dst is sb.BBH:
                    t_bb = t_ev
                else:
                    t_cc = t_ev
            if j8 % 2 == 1:
                sb.bc_war[qp] = t_mm
        t_y = None
        for i in range(KT):
            scan_toks = []
            for j8 in range(8):
                j = jh * 8 + j8
                a_ = (i * 8 + j8) % len(sb.ab)
                ctx.S.wait_all([sb.ab_war[a_], t_dt, A_tok])
                t_ab = ctx.S.done(lambda eng, sb=sb, a_=a_, i=i, j=j, Asb=Asb, TC=TC:
                                  eng.activation(sb.ab[a_][:], sb.DT[:, i * TC:(i + 1) * TC],
                                                 AF.Exp, scale=Asb[:, i * 16 + j:i * 16 + j + 1]))
                ctx.V.wait_all([sb.bb_war[a_], t_u, t_bb])
                t_b = ctx.V.done(lambda eng, sb=sb, a_=a_, i=i, j8=j8, TC=TC:
                                 eng.tensor_mul(sb.bb[a_][:], sb.U[:, i * TC:(i + 1) * TC],
                                                sb.BBH[:, j8 * TC:(j8 + 1) * TC]))
                ctx.V.wait_all([t_ab, t_b, sb.carry_tok, hj_war])
                init = 0.0 if first else sb.carry[:, i * 16 + j:i * 16 + j + 1]
                t_sc = ctx.V.done(lambda eng, sb=sb, a_=a_, j8=j8, init=init, TC=TC:
                                  eng.tensor_tensor_scan(sb.HJ[:, j8 * TC:(j8 + 1) * TC],
                                                         sb.ab[a_][:], sb.bb[a_][:], init,
                                                         ALU.mult, ALU.add))
                scan_toks.append(t_sc)
                sb.ab_war[a_] = t_sc
                sb.bb_war[a_] = t_sc
            ctx.V.wait(scan_toks[-1])
            carry_new = ctx.V.done(lambda eng, sb=sb, i=i, jh=jh, TC=TC:
                                   eng.tensor_copy(
                                       sb.carry[:, i * 16 + jh * 8:i * 16 + jh * 8 + 8],
                                       sb.HJ[:, TC - 1::TC]))
            ctx.V.wait_all([t_cc, pb_war])
            t_p = ctx.V.done(lambda eng, sb=sb: eng.tensor_mul(sb.PB[:], sb.HJ[:], sb.CBH[:]))
            hj_war = t_p
            ctx.V.wait(t_p)
            if jh == 0:
                def _red(eng, sb=sb, i=i, TC=TC, ctx=ctx):
                    with ctx.nc.allow_low_precision("bf16 half-sum output; DVE reduce is fp32 internal"):
                        return eng.reduce_sum(sb.YH[:, i * TC:(i + 1) * TC],
                                              sb.PB[:].rearrange("p (a b) -> p b a", b=TC),
                                              axis=AX.X)
                t_y = ctx.V.done(_red)
            else:
                ysel = i % 2
                ctx.V.wait(sb.yh_war[ysel])
                t_y = ctx.V.done(lambda eng, sb=sb, ysel=ysel, TC=TC:
                                 eng.reduce_sum(sb.yh[ysel][:],
                                                sb.PB[:].rearrange("p (a b) -> p b a", b=TC),
                                                axis=AX.X))
                ctx.V.wait(t_y)
                t_ya = ctx.V.done(lambda eng, sb=sb, ysel=ysel, i=i, TC=TC:
                                  eng.tensor_add(sb.yh[ysel][:], sb.yh[ysel][:],
                                                 sb.YH[:, i * TC:(i + 1) * TC]))
                ctx.V.wait_all([t_ya, t_z])
                yz_tok = ctx.V.done(lambda eng, sb=sb, ysel=ysel, i=i, TC=TC:
                                    eng.tensor_mul(sb.YZ[:, i * TC:(i + 1) * TC],
                                                   sb.yh[ysel][:], sb.Z[:, i * TC:(i + 1) * TC]))
                sb.yh_war[ysel] = yz_tok
            pb_war = t_y
        half_guard = t_y
    sb.carry_tok = carry_new

    rhs = [sb.YZ[:, i * TC:(i + 1) * TC] for i in range(KT)]

    def evac_o(bank_list, mm_tok):
        ctx.V.wait(mm_tok)
        last = None
        for (bi, ps, mi) in bank_list:
            last = ctx.V.done(lambda eng, ps=ps, mi=mi, R=R, cs=cs:
                              eng.scalar_tensor_tensor(R[mi][:, cs], ps[:, :], 0.0,
                                                       R[mi][:, cs], ALU.bypass, ALU.add))
        return last
    t_o = emit_linear_krows(ctx, pp, ws, wout, D, D, rhs, yz_tok, evac_o,
                            bias_rank1=(lambda mi: boutrow[0:1, mi * 128:mi * 128 + 128],
                                        com.onesrowb, com.ones_tok))
    return t_o


def build_l1a(T=2048, TC=512):
    nc = bass.Bass("TRN2", target_bir_lowering=False, debug=False)
    xin = nc.dram_tensor("xT", [D, T], F32, kind="ExternalInput").ap()
    dws = {}
    for nm, dout in (("Wz", D), ("Wx", D), ("Wdt", D), ("Wout", D), ("WB", 16), ("WC", 16)):
        dws[nm] = nc.dram_tensor(nm, [D, dout], F32, kind="ExternalInput").ap()
    dbs = {}
    for nm, n in (("bz", D), ("bx", D), ("bdt", D), ("bout", D), ("bB", 16), ("bC", 16)):
        dbs[nm] = nc.dram_tensor(nm, [n], F32, kind="ExternalInput").ap()
    alog = nc.dram_tensor("A_log", [D, 16], F32, kind="ExternalInput").ap()
    xout = nc.dram_tensor("yT", [D, T], F32, kind="ExternalOutput").ap()
    ctx = Ctx(nc, T, TC)
    with ctx.es:
        banks = [ctx.psum("bank", [128, TC]) for _ in range(8)]
        pp = PsumPool(banks)
        com = Common(ctx)
        com.c(1.0); com.c(1e-5); com.c(-0.5)
        com.finish_setup()
        ln = LNState(ctx)
        XN = ctx.sbuf("XN", [128, KT * TC], BF16)
        sb = SSMBufs(ctx, pb=XN)
        ws = KRowStream(ctx, "ws", D)
        wB = WResident(ctx, dws["WB"], D, 16, name="wB")
        wC = WResident(ctx, dws["WC"], D, 16, name="wC")
        bz, _ = load_bias_cols(ctx, pp, com, dbs["bz"], D, name="bz")
        bx, _ = load_bias_cols(ctx, pp, com, dbs["bx"], D, name="bx")
        bdt, _ = load_bias_cols(ctx, pp, com, dbs["bdt"], D, name="bdt")
        boutrow, _ = load_bias_row(ctx, com, dbs["bout"], D, name="bout", dtype=BF16)
        bB16 = ctx.sbuf("bB16", [16, 1], F32)
        bC16 = ctx.sbuf("bC16", [16, 1], F32)
        ctx.hdma(com.setup_ch, bB16[:], dbs["bB"].rearrange("(a b) -> a b", b=1))
        ctx.hdma(com.setup_ch, bC16[:], dbs["bC"].rearrange("(a b) -> a b", b=1))
        Araw = ctx.sbuf("Araw", [128, 128], F32)
        for ii in range(KT):
            ctx.hdma(com.setup_ch, Araw[:, ii * 16:(ii + 1) * 16],
                     alog[ii * 128:(ii + 1) * 128, :])
        ta = com.setup_tok
        Asb = ctx.sbuf("Asb", [128, 128], F32)
        ctx.S.wait(ta)
        tae = ctx.S.done(lambda eng: eng.activation(Asb[:], Araw[:], AF.Exp))
        ctx.V.wait(tae)
        A_tok = ctx.V.done(lambda eng: eng.tensor_scalar_mul(Asb[:], Asb[:], -1.0))
        R = [ctx.sbuf("R", [128, T], F32) for _ in range(KT)]
        xn = [XN[:, i * TC:(i + 1) * TC] for i in range(KT)]
        ch_r = ctx.channel("rload")
        r_toks = [ctx.hdma(ch_r, R[i][:], xin[i * 128:(i + 1) * 128, :]) for i in range(KT)]
        r_tok = r_toks[-1]
        ctx.barrier()
        for c in range(ctx.NCH):
            t_ln = emit_ln_chunk(ctx, pp, com, ln, R, r_tok, c, xn)
            r_tok = emit_l1a_chunk(ctx, pp, com, ln, sb, ws, xn, t_ln, R, c,
                                   dws["Wz"], dws["Wx"], dws["Wdt"], dws["Wout"],
                                   wB, wC, bz, bx, bdt, bB16, bC16, boutrow,
                                   Asb, A_tok, first=(c == 0))
        ctx.barrier()
        ch_o = ctx.channel("store")
        st = [ctx.hdma(ch_o, xout[i * 128:(i + 1) * 128, :], R[i][:]) for i in range(KT)]
        ctx.Y.wait(st[-1])
        with nc.Block() as block:
            ctx.build(block)
    return nc



class AttnBufs:
    def __init__(self, ctx):
        TC, SUB = ctx.TC, ctx.SUB
        self.Q = ctx.sbuf("Qb", [128, KT * TC], BF16)
        self.KV = ctx.sbuf("KVb", [128, 16 * TC], BF16)
        self.LAT = ctx.sbuf("LAT", [128, 2 * TC], BF16)
        if getattr(ctx, "_scr_on", False):
            self.vflat = ctx.scr(BF16, 16, 64 * SUB, base=0, share="att", share_elems=64 * SUB)
            self.qbig = [ctx.scr(BF16, 64, 16 * SUB, base=64, share="att", share_off=j * 16 * SUB)
                         for j in (0, 2)]
            self.kbig = [ctx.scr(BF16, 64, 16 * SUB, base=64, share="att", share_off=j * 16 * SUB)
                         for j in (1, 3)]
        else:
            self.qbig = [ctx.sbuf("qbig", [64, 16 * SUB], BF16) for _ in range(2)]
            self.kbig = [ctx.sbuf("kbig", [64, 16 * SUB], BF16) for _ in range(2)]
            self.vflat = ctx.sbuf("vflat", [16, 64 * SUB], BF16)
        self.Ebuf = [ctx.sbuf("Ebuf", [16, 16 * SUB], BF16) for _ in range(2)]
        self.AOB = ctx.sbuf("AOB", [64, 16 * TC], BF16)
        self.Zr = ctx.sbuf("Zr", [1, 16 * SUB], F32)
        self.ch_q = [ctx.channel("chq0"), ctx.channel("chq1")]
        self.ch_k = [ctx.channel("chk0"), ctx.channel("chk1")]
        self.ch_v = ctx.channel("chv")
        self.wro = None
        self.qbig_war = [None, None]
        self.kbig_war = [None, None]
        self.vflat_war = None
        self.ebuf_war = [None, None]
        self.zr_war = None
        self.aob_war = None


def emit_l0a_chunk(ctx, pp, com, ab, ws, ws64, xn, xn_tok, R, c,
                   wqd, wdown, wup, woutd, bq, bdn, bup, boutrow):
    TC, SUB = ctx.TC, ctx.SUB
    cs = slice(c * TC, (c + 1) * TC)
    nsub = TC // SUB

    def evac_q(bank_list, mm_tok):
        ctx.S.wait(mm_tok)
        last = None
        for (bi, ps, mi) in bank_list:
            last = ctx.S.done(lambda eng, ab=ab, ps=ps, mi=mi, b=bq, TC=TC:
                              eng.activation(ab.Q[:, mi * TC:(mi + 1) * TC], ps[:, :],
                                             AF.Identity, bias=b[:, mi:mi + 1]))
        return last
    t_q = emit_linear_krows(ctx, pp, ws, wqd, D, D, xn, xn_tok, evac_q)
    if getattr(ab, "wro", None) is not None:
        ctx.V.wait_all([t_q, ab.wro_tok])
        for mi in range(KT):
            t_q = ctx.V.done(lambda eng, ab=ab, mi=mi, c=c, TC=TC:
                             eng.tensor_mul(ab.Q[:, mi * TC:(mi + 1) * TC],
                                            ab.Q[:, mi * TC:(mi + 1) * TC],
                                            ab.wro[0:128, c * TC:(c + 1) * TC]))

    def evac_l(bi, ps, mi, mw, mm_tok):
        ctx.S.wait(mm_tok)
        return ctx.S.done(lambda eng, ab=ab, ps=ps, mi=mi, b=bdn, TC=TC:
                          eng.activation(ab.LAT[:, mi * TC:(mi + 1) * TC], ps[:, :],
                                         AF.Identity, bias=b[:, mi:mi + 1]))
    t_l = emit_linear_resident(ctx, pp, wdown, xn, xn_tok, evac_l)

    lat_aps = [ab.LAT[:, 0 * TC:(0 + 1) * TC], ab.LAT[:, 1 * TC:(1 + 1) * TC]]
    def evac_kv(bi, ps, mi, mw, mm_tok):
        ctx.S.wait(mm_tok)
        return ctx.S.done(lambda eng, ab=ab, ps=ps, mi=mi, b=bup, TC=TC:
                          eng.activation(ab.KV[:, mi * TC:(mi + 1) * TC], ps[:, :],
                                         AF.Identity, bias=b[:, mi:mi + 1]))
    t_kv = emit_linear_resident(ctx, pp, wup, lat_aps, t_l, evac_kv)

    c0125 = com.c(0.125, p=16)
    aob_tok = None

    def rebuild(sub):
        par = sub % 2
        s0 = sub * SUB
        qb4 = ab.qbig[par][:, :].rearrange("p (j q t) -> p j q t", q=2, t=SUB)
        t_rq = None
        for pq in range(2):
            t_rq = ctx.gdma(ab.ch_q[par], qb4[:, :, pq, :],
                            ab.Q[pq * 64:pq * 64 + 64, :].rearrange(
                                "p (j t) -> p j t", t=TC)[:, :, s0:s0 + SUB],
                            wait=[ab.qbig_war[par], t_q] if pq == 0 else ())
        kv3 = ab.KV[0:64, :].rearrange("p (g t) -> p g t", t=TC)
        t_rk = ctx.gdma(ab.ch_k[par], ab.kbig[par][:, :].rearrange("p (g t) -> p g t", t=SUB),
                        kv3[:, :, s0:s0 + SUB], wait=[ab.kbig_war[par], t_kv])
        t_rv = None
        for g in range(16):
            t_rv = ctx.gdma(ab.ch_v, ab.vflat[g:g + 1, :],
                            ab.KV[64:128, g * TC + s0:g * TC + s0 + SUB],
                            wait=[ab.vflat_war, t_kv] if g == 0 else ())
        return t_rq, t_rk, t_rv

    toks = rebuild(0)
    for sub in range(nsub):
        par = sub % 2
        s0 = sub * SUB
        t_rq, t_rk, t_rv = toks
        BW = TC
        nbg = SUB * 16 // BW
        tpb = BW // 16
        att_banks = pp.acquire_n(nbg)
        ctx.Te.wait_all([w for (_, _, w) in att_banks] + [t_rq, t_rk])
        qk_toks = []
        for bg in range(nbg):
            _, ps, _ = att_banks[bg]
            for tt in range(tpb):
                tok_i = bg * tpb + tt
                f = (lambda eng, ps=ps, ab=ab, tt=tt, tok_i=tok_i, SUB=SUB, par=par:
                     eng.matmul(ps[0:16, tt * 16:(tt + 1) * 16],
                                ab.kbig[par][:, tok_i::SUB], ab.qbig[par][:, tok_i::SUB],
                                start=True, stop=True))
                if tt == tpb - 1:
                    qk_toks.append(ctx.Te.done(f))
                else:
                    ctx.Te.raw(f)
        ab.qbig_war[par] = qk_toks[-1]
        ab.kbig_war[par] = qk_toks[-1]
        e_toks = []
        for bg in range(nbg):
            _, ps, _ = att_banks[bg]
            ctx.S.wait_all([qk_toks[bg], ab.ebuf_war[par]])
            e_toks.append(ctx.S.done(
                lambda eng, ab=ab, ps=ps, bg=bg, sc=c0125, BW=BW, par=par:
                eng.activation(ab.Ebuf[par][0:16, bg * BW:(bg + 1) * BW], ps[0:16, :],
                               AF.Exp, scale=sc)))
        for bg in range(nbg):
            pp.release(att_banks[bg][0], e_toks[bg])
        zbanks = pp.acquire_n(nbg)
        zm_toks = []
        for bg in range(nbg):
            _, ps_z, war_z = zbanks[bg]
            ctx.Te.wait_all([war_z, e_toks[bg], com.ones_tok])
            zm_toks.append(ctx.Te.done(
                lambda eng, ps=ps_z, com=com, ab=ab, bg=bg, BW=BW, par=par:
                eng.matmul(ps[0:1, :], com.ones16[:, :],
                           ab.Ebuf[par][0:16, bg * BW:(bg + 1) * BW],
                           start=True, stop=True)))
        zr_toks = []
        for bg in range(nbg):
            _, ps_z, _ = zbanks[bg]
            ctx.V.wait_all([zm_toks[bg], ab.zr_war if bg == 0 else None])
            zr_toks.append(ctx.V.done(
                lambda eng, ab=ab, ps=ps_z, bg=bg, BW=BW:
                eng.reciprocal(ab.Zr[0:1, bg * BW:(bg + 1) * BW], ps[0:1, :])))
        for bg in range(nbg):
            pp.release(zbanks[bg][0], zr_toks[bg])
        bbanks = pp.acquire_n(nbg)
        zb_toks = []
        for bg in range(nbg):
            _, ps_b, war_b = bbanks[bg]
            ctx.Te.wait_all([war_b, zr_toks[bg]])
            zb_toks.append(ctx.Te.done(
                lambda eng, ps=ps_b, com=com, ab=ab, bg=bg, BW=BW:
                eng.matmul(ps[0:16, :], com.onesrow[0:1, 0:16],
                           ab.Zr[0:1, bg * BW:(bg + 1) * BW],
                           start=True, stop=True)))
        nrm_toks = []
        for bg in range(nbg):
            _, ps_b, _ = bbanks[bg]
            ctx.V.wait(zb_toks[bg])
            t_nm = ctx.V.done(lambda eng, ab=ab, ps=ps_b, bg=bg, BW=BW, par=par:
                              eng.tensor_mul(ab.Ebuf[par][0:16, bg * BW:(bg + 1) * BW],
                                             ab.Ebuf[par][0:16, bg * BW:(bg + 1) * BW],
                                             ps[0:16, :]))
            pp.release(bbanks[bg][0], t_nm)
            nrm_toks.append(t_nm)
        ab.zr_war = zb_toks[-1]
        av_banks = pp.acquire_n(nbg)
        ctx.Te.wait_all([w for (_, _, w) in av_banks] + [t_rv] + nrm_toks)
        av_toks = []
        for bg in range(nbg):
            _, ps, _ = av_banks[bg]
            for tt in range(tpb):
                tok_i = bg * tpb + tt
                f = (lambda eng, ps=ps, ab=ab, tt=tt, tok_i=tok_i, SUB=SUB, par=par:
                     eng.matmul(ps[0:64, tt * 16:(tt + 1) * 16],
                                ab.vflat[:, tok_i::SUB],
                                ab.Ebuf[par][0:16, tok_i * 16:(tok_i + 1) * 16],
                                start=True, stop=True))
                if tt == tpb - 1:
                    av_toks.append(ctx.Te.done(f))
                else:
                    ctx.Te.raw(f)
        ab.ebuf_war[par] = av_toks[-1]
        ab.vflat_war = av_toks[-1]
        if sub + 1 < nsub:
            toks = rebuild(sub + 1)
        ev_toks = []
        for bg in range(nbg):
            _, ps, _ = av_banks[bg]
            ctx.V.wait_all([av_toks[bg], ab.aob_war])
            dst = ab.AOB[:, :].rearrange("p (h t) -> p h t", t=TC)[:, :, s0 + bg * tpb:s0 + (bg + 1) * tpb]
            srcv = ps[0:64, 0:BW].rearrange("p (t h) -> p h t", h=16)
            ev_toks.append(ctx.V.done(lambda eng, dst=dst, srcv=srcv:
                                      eng.tensor_copy(dst, srcv)))
        for bg in range(nbg):
            pp.release(av_banks[bg][0], ev_toks[bg])
        aob_tok = ev_toks[-1]
    ab.aob_war = aob_tok

    rhs = [ab.AOB[:, h * TC:(h + 1) * TC] for h in range(16)]

    def evac_o(bank_list, mm_tok):
        ctx.V.wait(mm_tok)
        last = None
        for (bi, ps, mi) in bank_list:
            last = ctx.V.done(lambda eng, ps=ps, mi=mi, R=R, cs=cs:
                              eng.scalar_tensor_tensor(R[mi][:, cs], ps[:, :], 0.0,
                                                       R[mi][:, cs], ALU.bypass, ALU.add))
        return last
    t_o = emit_linear_krows(ctx, pp, ws64, woutd, D, D, rhs, aob_tok, evac_o,
                            krows=64,
                            bias_rank1=(lambda mi: boutrow[0:1, mi * 128:mi * 128 + 128],
                                        com.onesrowb, com.ones_tok))
    ab.aob_war = t_o
    return t_o


def build_l0a(T=2048, TC=512, SUB=128):
    nc = bass.Bass("TRN2", target_bir_lowering=False, debug=False)
    xin = nc.dram_tensor("xT", [D, T], F32, kind="ExternalInput").ap()
    wq = nc.dram_tensor("Wq", [D, D], F32, kind="ExternalInput").ap()
    wdn = nc.dram_tensor("Wdown", [D, 256], F32, kind="ExternalInput").ap()
    wupd = nc.dram_tensor("Wup", [256, 2048], F32, kind="ExternalInput").ap()
    wo = nc.dram_tensor("Wout", [D, D], F32, kind="ExternalInput").ap()
    dbs = {}
    for nm, n in (("bq", D), ("bdown", 256), ("bup", 2048), ("bout", D)):
        dbs[nm] = nc.dram_tensor(nm, [n], F32, kind="ExternalInput").ap()
    xout = nc.dram_tensor("yT", [D, T], F32, kind="ExternalOutput").ap()
    ctx = Ctx(nc, T, TC, SUB=SUB)
    with ctx.es:
        banks = [ctx.psum("bank", [128, TC]) for _ in range(8)]
        pp = PsumPool(banks)
        com = Common(ctx)
        com.c(1e-5); com.c(-0.5); com.c(0.125, p=16)
        com.finish_setup()
        ln = LNState(ctx)
        ab = AttnBufs(ctx)
        ws = KRowStream(ctx, "ws", D)
        ws64 = KRowStream(ctx, "ws64", D, krows=64)
        wdown = WResident(ctx, wdn, D, 256, name="wdn")
        wup = WResident(ctx, wupd, 256, 2048, name="wup")
        bq, _ = load_bias_cols(ctx, pp, com, dbs["bq"], D, name="bq")
        bdn, _ = load_bias_cols(ctx, pp, com, dbs["bdown"], 256, name="bdn")
        bup, _ = load_bias_cols(ctx, pp, com, dbs["bup"], 2048, name="bup")
        boutrow, _ = load_bias_row(ctx, com, dbs["bout"], D, name="bout", dtype=BF16)
        R = [ctx.sbuf("R", [128, T], F32) for _ in range(KT)]
        XN = ctx.sbuf("XN", [128, KT * TC], BF16)
        xn = [XN[:, i * TC:(i + 1) * TC] for i in range(KT)]
        ch_r = ctx.channel("rload")
        r_toks = [ctx.hdma(ch_r, R[i][:], xin[i * 128:(i + 1) * 128, :]) for i in range(KT)]
        r_tok = r_toks[-1]
        ctx.barrier()
        for c in range(ctx.NCH):
            t_ln = emit_ln_chunk(ctx, pp, com, ln, R, r_tok, c, xn)
            r_tok = emit_l0a_chunk(ctx, pp, com, ab, ws, ws64, xn, t_ln, R, c,
                                   wq, wdown, wup, wo, bq, bdn, bup, boutrow)
        ctx.barrier()
        ch_o = ctx.channel("store")
        st = [ctx.hdma(ch_o, xout[i * 128:(i + 1) * 128, :], R[i][:]) for i in range(KT)]
        ctx.Y.wait(st[-1])
        with nc.Block() as block:
            ctx.build(block)
    return nc



class MoeBufs:
    def __init__(self, ctx, E=8):
        T, TC = ctx.T, ctx.TC
        self.E = E
        self.Lrow = ctx.sbuf("Lrow", [E, T], BF16)
        self.SCT = ctx.sbuf("SCT", [128, (T // 128) * E], F32)
        self.MKT = ctx.sbuf("MKT", [128, (T // 128) * E], F32)
        self.SET = ctx.sbuf("SET", [128, (T // 128) * E], F32)
        self.SErow = ctx.sbuf("SErow", [E, T], BF16)
        self.Mrow = ctx.sbuf("Mrow", [E, T], BF16)
        if getattr(ctx, "_scr_on", False):
            self.se0 = ctx.scr(BF16, 1, TC, base=0, share="sem0", share_elems=TC)
            self.m0 = ctx.scr(BF16, 1, TC, base=32, share="sem0")
            self.b2r = ctx.scr(BF16, 1, D, base=0, share="ber", share_elems=D)
            self.b3r = ctx.scr(BF16, 1, D, base=32, share="ber")
        else:
            self.se0 = ctx.sbuf("se0", [1, TC], BF16)
            self.m0 = ctx.sbuf("m0", [1, TC], BF16)
            self.b2r = ctx.sbuf("b2r", [1, D], BF16)
            self.b3r = ctx.sbuf("b3r", [1, D], BF16)
        self.mx8 = ctx.sbuf("mx8", [128, 8], F32)
        self.rz = ctx.sbuf("rz", [128, T // 128], F32)
        self.SEB = ctx.sbuf("SEB", [128, TC], BF16)
        self.lsc = ctx.sbuf("lsc", [E, 128], F32)
        self.X3 = ctx.sbuf("X3", [128, KT * TC], BF16)
        self.tmp = [ctx.sbuf("mtmp", [128, TC], BF16)]
        self.tmp_war = [None]
        self.ch_se = ctx.channel("chse")
        self.ex_war = None
        self.xnl_war = None
        self.sem_war = None
        self.seb_war = None
        self.xs_war = None
        self.h1_war = None
        self.x3_war = None


def emit_routing_logits(ctx, pp, com, mb, gw, xn, xn_tok, c):
    TC = ctx.TC
    E = mb.E

    def evac_lg(bi, ps, mi, mw, mm_tok):
        ctx.S.wait(mm_tok)
        return ctx.S.done(lambda eng, mb=mb, ps=ps, c=c, TC=TC, E=E:
                          eng.activation(mb.Lrow[:, c * TC:(c + 1) * TC], ps[0:E, :],
                                         AF.Identity))
    return emit_linear_resident(ctx, pp, gw, xn, xn_tok, evac_lg)


def emit_routing_finish(ctx, pp, com, mb, t_lg):
    T, TC = ctx.T, ctx.TC
    E = mb.E
    nt = T // 128
    t_tr = None
    t_ls = None
    for tt in range(nt):
        ctx.V.wait_all([t_lg, t_ls])
        t_cp = ctx.V.done(lambda eng, mb=mb, tt=tt:
                          eng.tensor_copy(mb.lsc[:], mb.Lrow[:, tt * 128:(tt + 1) * 128]))
        bi, ps, war = pp.acquire()
        ctx.Te.wait_all([war, t_cp, com.idf_tok])
        t_mm = ctx.Te.done(lambda eng, ps=ps, mb=mb, com=com, E=E:
                           eng.transpose(ps[0:128, 0:E], mb.lsc[:], com.idf[0:E, 0:E]))
        t_ls = t_mm
        ctx.V.wait(t_mm)
        t_tr = ctx.V.done(lambda eng, mb=mb, ps=ps, tt=tt, E=E:
                          eng.tensor_copy(mb.SCT[:, tt * E:(tt + 1) * E], ps[0:128, 0:E]))
        pp.release(bi, t_tr)
    ctx.S.wait(t_tr)
    t_exp = ctx.S.done(lambda eng, mb=mb: eng.activation(mb.SCT[:], mb.SCT[:], AF.Exp))
    ctx.V.wait(t_exp)
    t_z = ctx.V.done(lambda eng, mb=mb, nt=nt, E=E:
                     eng.reduce_sum(mb.rz[:], mb.SCT[:].rearrange("p (a b) -> p a b", b=E),
                                    axis=AX.X))
    ctx.V.wait(t_z)
    t_rz = ctx.V.done(lambda eng, mb=mb: eng.reciprocal(mb.rz[:], mb.rz[:]))
    last = None
    for tt in range(nt):
        ctx.V.wait(t_rz)
        t_n = ctx.V.done(lambda eng, mb=mb, tt=tt, E=E:
                         eng.tensor_scalar_mul(mb.SCT[:, tt * E:(tt + 1) * E],
                                               mb.SCT[:, tt * E:(tt + 1) * E],
                                               mb.rz[:, tt:tt + 1]))
        ctx.V.wait(t_n)
        t_m8 = ctx.V.done(lambda eng, mb=mb, tt=tt, E=E:
                          eng.max(mb.mx8[:], mb.SCT[:, tt * E:(tt + 1) * E]))
        ctx.V.wait(t_m8)
        t_mk = ctx.V.done(lambda eng, mb=mb, tt=tt, E=E:
                          eng.tensor_scalar(mb.MKT[:, tt * E:(tt + 1) * E],
                                            mb.SCT[:, tt * E:(tt + 1) * E],
                                            mb.mx8[:, 1:2], None, ALU.is_ge))
        ctx.V.wait(t_mk)
        last = ctx.V.done(lambda eng, mb=mb, tt=tt, E=E:
                          eng.tensor_mul(mb.SET[:, tt * E:(tt + 1) * E],
                                         mb.SCT[:, tt * E:(tt + 1) * E],
                                         mb.MKT[:, tt * E:(tt + 1) * E]))
    t_se = t_mrow = None
    for tt in range(nt):
        for (srcbuf, dstbuf) in ((mb.SET, mb.SErow), (mb.MKT, mb.Mrow)):
            bi, ps, war = pp.acquire()
            ctx.Te.wait_all([war, last, com.idf_tok])
            t_mm = ctx.Te.done(lambda eng, ps=ps, srcbuf=srcbuf, tt=tt, com=com, E=E:
                               eng.transpose(ps[0:E, 0:128], srcbuf[:, tt * E:(tt + 1) * E],
                                             com.idf[:, 0:128]))
            ctx.V.wait(t_mm)
            t_ev = ctx.V.done(lambda eng, dstbuf=dstbuf, ps=ps, tt=tt, E=E:
                              eng.tensor_copy(dstbuf[:, tt * 128:(tt + 1) * 128], ps[0:E, 0:128]))
            pp.release(bi, t_ev)
            if dstbuf is mb.SErow:
                t_se = t_ev
            else:
                t_mrow = t_ev
    return (t_se, t_mrow)


def emit_expert_chunk(ctx, pp, com, mb, w1, w3, ws, w2d, e, c, xnv, xn_tok,
                      b1c, se_tok, R, h1buf, xsbuf, b23=None):
    T, TC = ctx.T, ctx.TC
    cs = slice(c * TC, (c + 1) * TC)
    E = mb.E
    t_se, t_mrow = se_tok
    ctx.gdma(mb.ch_se, mb.se0[:], mb.SErow[e:e + 1, cs], wait=[t_se, t_mrow, mb.sem_war])
    t_m0 = ctx.gdma(mb.ch_se, mb.m0[:], mb.Mrow[e:e + 1, cs])
    if b23 is not None:
        b2d_row, b3d_row, ch_br = b23
        ctx.gdma(ch_br, mb.b2r[:], b2d_row, wait=[mb.h1_war])
        mb.b2r_tok = ctx.gdma(ch_br, mb.b3r[:], b3d_row)
    bi, ps, war = pp.acquire()
    ctx.Te.wait_all([war, t_m0, com.ones_tok])
    t_mm = ctx.Te.done(lambda eng, ps=ps, com=com, mb=mb, TC=TC:
                       eng.matmul(ps[:, :], com.onesrowb[0:1, 0:128],
                                  mb.se0[0:1, :], start=True, stop=True))
    ctx.S.wait_all([t_mm, mb.seb_war])
    t_seb = ctx.S.done(lambda eng, mb=mb, ps=ps: eng.activation(mb.SEB[:], ps[:, :], AF.Copy))
    pp.release(bi, t_seb)
    xn_toks = list(xn_tok) if isinstance(xn_tok, tuple) else [xn_tok]
    xs_tok = None
    for i in range(KT):
        ctx.V.wait_all([t_seb, mb.xs_war] + xn_toks)
        xs_tok = ctx.V.done(lambda eng, mb=mb, xsbuf=xsbuf, xnv=xnv, i=i, TC=TC:
                            eng.tensor_mul(xsbuf[:, i * TC:(i + 1) * TC],
                                           xnv[i], mb.SEB[:]))
    mb.xnl_war = xs_tok
    mb.seb_war = xs_tok
    xs = [xsbuf[:, i * TC:(i + 1) * TC] for i in range(KT)]
    def evac_g1(bi, ps, mi, mw, mm_tok):
        ctx.S.wait_all([mm_tok, mb.h1_war])
        return ctx.S.done(lambda eng, h1buf=h1buf, ps=ps, mi=mi, b=b1c, e=e, TC=TC:
                          eng.activation(h1buf[:, mi * TC:(mi + 1) * TC], ps[:, :],
                                         AF.Silu, bias=b[:, mi:mi + 1]))
    t_h1 = emit_linear_resident(ctx, pp, w1, xs, xs_tok, evac_g1)
    def evac_x3(bi, ps, mi, mw, mm_tok):
        ctx.S.wait_all([mm_tok, mb.x3_war])
        return ctx.S.done(lambda eng, mb=mb, ps=ps, mi=mi, TC=TC:
                          eng.activation(mb.X3[:, mi * TC:(mi + 1) * TC], ps[:, :], AF.Copy))
    ctx.Te.wait(mb.b2r_tok)
    t_x3 = emit_linear_resident(
        ctx, pp, w3, xs, xs_tok, evac_x3,
        extra_k=[(lambda mi: mb.b3r[0:1, mi * 128:(mi + 1) * 128],
                  mb.m0[0:1, :], t_m0)])
    mb.w13_reader = ctx.Te.tok()
    mb.sem_war = t_x3
    mb.xs_war = t_x3
    rhs = [h1buf[:, kh * TC:(kh + 1) * TC] for kh in range(KT)]

    def evac_o(bank_list, mm_tok):
        mb.h1_war = mm_tok
        ctx.V.wait_all([mm_tok, t_x3])
        last = None
        t_p = None
        for (bi, ps, mi) in bank_list:
            ts_ = mi % len(mb.tmp)
            ctx.V.wait(mb.tmp_war[ts_])
            t_p = ctx.V.done(lambda eng, mb=mb, ts_=ts_, ps=ps, mi=mi, TC=TC:
                             eng.tensor_mul(mb.tmp[ts_][:], ps[:, :],
                                            mb.X3[:, mi * TC:(mi + 1) * TC]))
            ctx.V.wait_all([t_p, mb.r_war])
            last = ctx.V.done(lambda eng, mb=mb, ts_=ts_, mi=mi, R=R, cs=cs:
                              eng.tensor_add(R[mi][:, cs], R[mi][:, cs], mb.tmp[ts_][:]))
            mb.tmp_war[ts_] = last
            mb.r_war = last
        mb.x3_war = t_p
        return last
    tok = emit_linear_krows(
        ctx, pp, ws, w2d, D, D, rhs, t_h1, evac_o,
        bias_rank1=(lambda mi: mb.b2r[0:1, mi * 128:(mi + 1) * 128],
                    com.onesrowb, mb.b2r_tok))
    mb.ex_war = tok
    return tok


def build_l1f(T=2048, TC=512, E=8, mgw=128):
    H = D
    I = 2048
    nc = bass.Bass("TRN2", target_bir_lowering=False, debug=False)
    xin = nc.dram_tensor("xT", [D, T], F32, kind="ExternalInput").ap()
    gwd = nc.dram_tensor("gwT", [D, E], F32, kind="ExternalInput").ap()
    w1d = nc.dram_tensor("w1T", [E, D, H], F32, kind="ExternalInput").ap()
    w2d = nc.dram_tensor("w2T", [E, H, D], F32, kind="ExternalInput").ap()
    w3d = nc.dram_tensor("w3T", [E, D, H], F32, kind="ExternalInput").ap()
    b1d = nc.dram_tensor("b1", [E, H], F32, kind="ExternalInput").ap()
    b2d = nc.dram_tensor("b2", [E, D], F32, kind="ExternalInput").ap()
    b3d = nc.dram_tensor("b3", [E, D], F32, kind="ExternalInput").ap()
    sw1d = nc.dram_tensor("sw1T", [D, I], F32, kind="ExternalInput").ap()
    sw2d = nc.dram_tensor("sw2T", [D, I], F32, kind="ExternalInput").ap()
    sw3d = nc.dram_tensor("sw3T", [I, D], F32, kind="ExternalInput").ap()
    xout = nc.dram_tensor("yT", [D, T], F32, kind="ExternalOutput").ap()
    ctx = Ctx(nc, T, TC)
    with ctx.es:
        banks = [ctx.psum("bank", [128, TC]) for _ in range(8)]
        pp = PsumPool(banks)
        com = Common(ctx)
        com.c(1e-5); com.c(-0.5); com.c(1.0)
        com.finish_setup()
        ln = LNState(ctx)
        mb = MoeBufs(ctx, E=E)
        mb.ex_war = None
        mb.r_war = None
        sw = SwigluBufs(ctx, I, mgw)
        gw = WResident(ctx, gwd, D, E, name="gw")
        b1cs = []
        for e in range(E):
            b1c, _ = load_bias_cols(ctx, pp, com,
                                    b1d[e:e + 1, :].rearrange("o (a b) -> (o a) b", b=128),
                                    H, name=f"b1c{e}", shaped=True)
            b1cs.append(b1c)

        R = [ctx.sbuf("R", [128, T], F32) for _ in range(KT)]
        XNF = ctx.sbuf("XNF", [128, KT * T], BF16)
        xn_chunks = [[XNF[:, i * T + c * TC:i * T + (c + 1) * TC] for i in range(KT)]
                     for c in range(ctx.NCH)]
        ch_r = ctx.channel("rload")
        r_toks = [ctx.hdma(ch_r, R[i][:], xin[i * 128:(i + 1) * 128, :]) for i in range(KT)]
        r_tok = r_toks[-1]
        ctx.barrier()
        xn_toks = []
        t_lg = None
        for c in range(ctx.NCH):
            xn_toks.append(emit_ln_chunk(ctx, pp, com, ln, R, r_tok, c, xn_chunks[c]))
            t_lg = emit_routing_logits(ctx, pp, com, mb, gw, xn_chunks[c], xn_toks[c], c)
        se_tok = emit_routing_finish(ctx, pp, com, mb, t_lg)
        for c in range(ctx.NCH):
            t_sh = emit_swiglu_chunk(ctx, pp, com, sw, xn_chunks[c], xn_toks[c],
                                     sw1d, sw2d, sw3d, None, None, R, c)
            mb.r_war = t_sh
        h1buf = sw.hbuf[:, 0:KT * TC]
        xsbuf = sw.hbuf[:, KT * TC:2 * KT * TC]
        w1sb = ctx.sbuf("w1sb", [128, KT * H], BF16)
        w3sb = ctx.sbuf("w3sb", [128, KT * H], BF16)
        ch_w1 = ctx.channel("chw1")
        ch_w3 = ctx.channel("chw3")
        ws2 = sw.w3s
        wtok = None
        mb.ex_war = mb.r_war
        mb.xs_war = mb.r_war
        mb.h1_war = mb.r_war
        ch_br = ctx.channel("chbr")
        for e in range(E):
            w1 = WResident(ctx, w1d[e], D, H, name=f"w1_{e}", sbuf=w1sb, ch=ch_w1, war=wtok)
            w3 = WResident(ctx, w3d[e], D, H, name=f"w3_{e}", sbuf=w3sb, ch=ch_w3, war=wtok)
            ctx.gdma(ch_br, mb.b2r[:], b2d[e:e + 1, :], wait=[wtok])
            mb.b2r_tok = ctx.gdma(ch_br, mb.b3r[:], b3d[e:e + 1, :])
            for c in range(ctx.NCH):
                xnv_c = [XNF[:, i * T + c * TC:i * T + (c + 1) * TC] for i in range(KT)]
                wtok = emit_expert_chunk(ctx, pp, com, mb, w1, w3, ws2, w2d[e], e, c,
                                         xnv_c, xn_toks[c], b1cs[e], se_tok,
                                         R, h1buf, xsbuf)
        ctx.barrier()
        ch_o = ctx.channel("store")
        st = [ctx.hdma(ch_o, xout[i * 128:(i + 1) * 128, :], R[i][:]) for i in range(KT)]
        ctx.Y.wait(st[-1])
        with nc.Block() as block:
            ctx.build(block)
    return nc



def build_fused(T=2048, TC=512, SUB=128, PB=40000, PF=6144, blocks=("l0a", "l0f", "l1a", "l1f")):
    E, HF, IS, DL = 8, 3072, 2048, 256
    nc = bass.Bass("TRN2", target_bir_lowering=False, debug=False)
    d = {}
    d["xT"] = nc.dram_tensor("xT", [D, T], F32, kind="ExternalInput").ap()
    for nm, shape, dt in (
        ("Wq", [D, D], F32), ("Wdown", [D, DL], F32), ("Wup", [DL, 2 * D], F32),
        ("Wout0", [D, D], F32), ("bq", [D], F32), ("bdown", [DL], F32),
        ("bup", [2 * D], F32), ("bout0", [D], F32),
        ("wgT", [D, HF], I8), ("wuT", [D, HF], I8), ("wdT", [HF, D], I8),
        ("Wz", [D, D], F32), ("Wx", [D, D], F32), ("Wdt", [D, D], F32),
        ("Wout1", [D, D], F32), ("WB", [D, 16], F32), ("WC", [D, 16], F32),
        ("bz", [D], F32), ("bx", [D], F32), ("bdt", [D], F32), ("bout1", [D], F32),
        ("bB", [16], F32), ("bC", [16], F32), ("A_log", [D, 16], F32),
        ("gwT", [D, E], F32), ("w1T", [E, D, D], F32), ("w2T", [E, D, D], F32),
        ("w3T", [E, D, D], F32), ("b1", [E, D], F32), ("b2", [E, D], F32),
        ("b3", [E, D], F32), ("sw1T", [D, IS], F32), ("sw2T", [D, IS], F32),
        ("sw3T", [IS, D], F32), ("Wro_hi", [128, T], F32), ("Wro_lo", [128, T], F32),
    ):
        d[nm] = nc.dram_tensor(nm, shape, dt, kind="ExternalInput").ap()
    yout = nc.dram_tensor("yT", [D, T], F32, kind="ExternalOutput").ap()

    ctx = Ctx(nc, T, TC, SUB=SUB)
    with ctx.es:
        banks = [ctx.psum("bank", [128, TC]) for _ in range(8)]
        pp = PsumPool(banks)
        com = Common(ctx)
        com.c(1e-5); com.c(-0.5); com.c(1.0); com.c(0.125, p=16)
        r127 = com.crecip(127.0)
        com.finish_setup()
        ln = LNState(ctx)
        ws = KRowStream(ctx, "ws", D)
        ws64 = KRowStream(ctx, "ws64", D, krows=64)
        bq, _ = load_bias_cols(ctx, pp, com, d["bq"], D, name="bq")
        bdn, _ = load_bias_cols(ctx, pp, com, d["bdown"], DL, name="bdn")
        bup, _ = load_bias_cols(ctx, pp, com, d["bup"], 2 * D, name="bup")
        bout0, _ = load_bias_row(ctx, com, d["bout0"], D, name="bout0", dtype=BF16)
        bz, _ = load_bias_cols(ctx, pp, com, d["bz"], D, name="bz")
        bx, _ = load_bias_cols(ctx, pp, com, d["bx"], D, name="bx")
        bdt, _ = load_bias_cols(ctx, pp, com, d["bdt"], D, name="bdt")
        bout1, _ = load_bias_row(ctx, com, d["bout1"], D, name="bout1", dtype=BF16)
        bB16 = ctx.sbuf("bB16", [16, 1], F32)
        bC16 = ctx.sbuf("bC16", [16, 1], F32)
        ctx.hdma(com.setup_ch, bB16[:], d["bB"].rearrange("(a b) -> a b", b=1))
        ctx.hdma(com.setup_ch, bC16[:], d["bC"].rearrange("(a b) -> a b", b=1))
        b1cs = []
        for e in range(E):
            b1c, _ = load_bias_cols(ctx, pp, com,
                                    d["b1"][e:e + 1, :].rearrange("o (a b) -> (o a) b", b=128),
                                    D, name=f"b1c{e}", shaped=True)
            b1cs.append(b1c)
        R = [ctx.sbuf("R", [128, T], F32) for _ in range(KT)]
        XN = ctx.sbuf("XN", [128, KT * TC], BF16)
        xn = [XN[:, i * TC:(i + 1) * TC] for i in range(KT)]
        ctx.scratch_init(PB, PF)
        ch_r = ctx.channel("rload")
        r_toks = [ctx.hdma(ch_r, R[i][:], d["xT"][i * 128:(i + 1) * 128, :]) for i in range(KT)]
        r_tok = r_toks[-1]
        ctx.barrier()

        W1B = nc.dram_tensor("w1b", [E, D, D], BF16, kind="Internal").ap()
        W2B = nc.dram_tensor("w2b", [E, D, D], BF16, kind="Internal").ap()
        W3B = nc.dram_tensor("w3b", [E, D, D], BF16, kind="Internal").ap()
        SW1B = nc.dram_tensor("sw1b", [D, IS], BF16, kind="Internal").ap()
        SW2B = nc.dram_tensor("sw2b", [D, IS], BF16, kind="Internal").ap()
        SW3B = nc.dram_tensor("sw3b", [IS, D], BF16, kind="Internal").ap()
        ch_bn = [ctx.channel(f"bounce{i}") for i in range(8)]
        _bn = [0]
        _bnq = [(SW1B[:, :], d["sw1T"]), (SW2B[:, :], d["sw2T"]), (SW3B[:, :], d["sw3T"])]
        for e in range(E):
            _bnq += [(W1B[e], d["w1T"][e]), (W2B[e], d["w2T"][e]), (W3B[e], d["w3T"][e])]

        def pump_bounce(k):
            while k > 0 and _bnq:
                dst, srcd = _bnq.pop(0)
                ctx.gdma(ch_bn[_bn[0] % 8], dst, srcd)
                _bn[0] += 1
                k -= 1

        def emit_bounce():
            pump_bounce(len(_bnq))

        if "l0a" in blocks:
            ctx.scratch_begin()
            ab = AttnBufs(ctx)
            wdown = WResident(ctx, d["Wdown"], D, DL, name="wdn")
            wup = WResident(ctx, d["Wup"], DL, 2 * D, name="wup")
            wro = ctx.sbuf("wro", [128, T], F32)
            wrol = ctx.sbuf("wrol", [128, T], F32)
            ctx.scratch_end()
            ch_w = ctx.channel("wroload")
            ctx.hdma(ch_w, wro[0:128, :], d["Wro_hi"][:])
            t_wl = ctx.hdma(ch_w, wrol[0:128, :], d["Wro_lo"][:])
            ctx.V.wait(t_wl)
            wro_tok = ctx.V.done(lambda eng, a=wro, b=wrol:
                                 eng.tensor_add(a[0:128, :], a[0:128, :], b[0:128, :]))
            ab.wro = wro
            ab.wro_tok = wro_tok
            for c in range(ctx.NCH):
                t_ln = emit_ln_chunk(ctx, pp, com, ln, R, r_tok, c, xn)
                r_tok = emit_l0a_chunk(ctx, pp, com, ab, ws, ws64, xn, t_ln, R, c,
                                       d["Wq"], wdown, wup, d["Wout0"], bq, bdn, bup, bout0)
                if "l1f" in blocks:
                    pump_bounce(3)
            ctx.barrier()

        if "l0f" in blocks:
            ctx.scratch_begin()
            sw = SwigluBufs(ctx, HF, 256, w3s=ws)
            ctx.scratch_end()
            for c in range(ctx.NCH):
                t_ln = emit_ln_chunk(ctx, pp, com, ln, R, r_tok, c, xn)
                r_tok = emit_swiglu_chunk(ctx, pp, com, sw, xn, t_ln,
                                          d["wgT"], d["wuT"], d["wdT"], r127, r127, R, c)
                if "l1f" in blocks:
                    pump_bounce(4)
            ctx.barrier()

        if "l1a" in blocks:
            ctx.scratch_begin()
            sb = SSMBufs(ctx, pb=XN)
            wB = WResident(ctx, d["WB"], D, 16, name="wB")
            wC = WResident(ctx, d["WC"], D, 16, name="wC")
            Araw = ctx.sbuf("Araw", [128, 128], F32)
            Asb = Araw
            ctx.scratch_end()
            ch_a = ctx.channel("aload")
            for ii in range(KT):
                ta = ctx.hdma(ch_a, Araw[:, ii * 16:(ii + 1) * 16],
                              d["A_log"][ii * 128:(ii + 1) * 128, :])
            ctx.S.wait(ta)
            tae = ctx.S.done(lambda eng: eng.activation(Asb[:], Araw[:], AF.Exp))
            ctx.V.wait(tae)
            A_tok = ctx.V.done(lambda eng: eng.tensor_scalar_mul(Asb[:], Asb[:], -1.0))
            if "l1f" in blocks:
                emit_bounce()
            for c in range(ctx.NCH):
                t_ln = emit_ln_chunk(ctx, pp, com, ln, R, r_tok, c, xn)
                r_tok = emit_l1a_chunk(ctx, pp, com, ln, sb, ws, xn, t_ln, R, c,
                                       d["Wz"], d["Wx"], d["Wdt"], d["Wout1"],
                                       wB, wC, bz, bx, bdt, bB16, bC16, bout1,
                                       Asb, A_tok, first=(c == 0))
            ctx.barrier()

        if "l1f" in blocks:
            if "l1a" not in blocks:
                emit_bounce()
                ctx.barrier()
            ctx.scratch_begin()
            mb = MoeBufs(ctx, E=E)
            mb.r_war = None
            sw2 = SwigluBufs(ctx, IS, 128, w3s=ws, nslots=1)
            gw = WResident(ctx, d["gwT"], D, E, name="gw")
            xnl = XN
            w1sb = ctx.sbuf("w1sb", [128, KT * D], BF16)
            w3sb = ctx.sbuf("w3sb", [128, KT * D], BF16)
            ctx.scratch_end()
            XNFD = nc.dram_tensor("xnfd", [D, T], BF16, kind="Internal").ap()
            ch_xf = ctx.channel("xnfst")
            ch_xl = ctx.channel("xnfld")
            ch_xl2 = ctx.channel("xnfld2")
            t_lg = None
            xst_tok = None
            for c in range(ctx.NCH):
                t_ln = emit_ln_chunk(ctx, pp, com, ln, R, r_tok, c, xn, xn_war=xst_tok)
                t_lg = emit_routing_logits(ctx, pp, com, mb, gw, xn, t_ln, c)
                t_sh = emit_swiglu_chunk(ctx, pp, com, sw2, xn, t_ln,
                                         SW1B, SW2B, SW3B, None, None, R, c)
                mb.r_war = t_sh
                xst_tok = ctx.hdma(
                    ch_xf,
                    XNFD[:, c * TC:(c + 1) * TC].rearrange("(a p) t -> p a t", p=128),
                    XN[:, :].rearrange("p (a t) -> p a t", a=KT), wait=[t_ln])
                r_tok = t_sh
            se_tok = emit_routing_finish(ctx, pp, com, mb, t_lg)
            h1buf = sw2.hbuf[:, 0:KT * TC]
            xsbuf = sw2.hbuf[:, KT * TC:2 * KT * TC]
            ch_w1 = ctx.channel("chw1")
            ch_w3 = ctx.channel("chw3")
            ch_br = ctx.channel("chbr")
            wtok = None
            mb.ex_war = mb.r_war
            mb.xnl_war = mb.r_war
            mb.sem_war = None
            mb.seb_war = None
            mb.xs_war = mb.r_war
            mb.h1_war = mb.r_war
            mb.x3_war = None
            xnv = [xnl[:, i * TC:(i + 1) * TC] for i in range(KT)]
            for e in range(E):
                w1 = WResident(ctx, d["w1T"][e], D, D, name=f"w1_{e}", sbuf=w1sb, ch=ch_w1, war=wtok)
                w3 = WResident(ctx, d["w3T"][e], D, D, name=f"w3_{e}", sbuf=w3sb, ch=ch_w3, war=wtok)
                ctx.gdma(ch_br, mb.b2r[:], d["b2"][e:e + 1, :], wait=[wtok])
                mb.b2r_tok = ctx.gdma(ch_br, mb.b3r[:], d["b3"][e:e + 1, :])
                for c in range(ctx.NCH):
                    hd = KT // 2
                    xl_a = ctx.hdma(
                        ch_xl, xnl[:, 0:hd * TC].rearrange("p (a t) -> p a t", a=hd),
                        XNFD[0:hd * 128, c * TC:(c + 1) * TC].rearrange(
                            "(a p) t -> p a t", p=128),
                        wait=[mb.xnl_war, xst_tok])
                    xl_b = ctx.hdma(
                        ch_xl2, xnl[:, hd * TC:].rearrange("p (a t) -> p a t", a=KT - hd),
                        XNFD[hd * 128:, c * TC:(c + 1) * TC].rearrange(
                            "(a p) t -> p a t", p=128),
                        wait=[mb.xnl_war, xst_tok])
                    xl_tok = (xl_a, xl_b)
                    wtok = emit_expert_chunk(ctx, pp, com, mb, w1, w3, ws, W2B[e], e, c,
                                             xnv, xl_tok, b1cs[e], se_tok,
                                             R, h1buf, xsbuf)
        ctx.barrier()
        ch_o = ctx.channel("store")
        st = [ctx.hdma(ch_o, yout[i * 128:(i + 1) * 128, :], R[i][:]) for i in range(KT)]
        ctx.Y.wait(st[-1])
        with nc.Block() as block:
            ctx.build(block)
    return ctx, nc



_CACHE = {}


def _ct(a):
    return np.ascontiguousarray(a)


def kernel(L0_attn_x, L0_attn_Wq, L0_attn_bq, L0_attn_Wdown, L0_attn_bdown,
           L0_attn_Wup, L0_attn_bup, L0_attn_Wout, L0_attn_bout,
           L0_ffn_w_gate, L0_ffn_w_up, L0_ffn_w_down,
           L1_attn_Wz, L1_attn_bz, L1_attn_Wx, L1_attn_bx, L1_attn_WB, L1_attn_bB,
           L1_attn_WC, L1_attn_bC, L1_attn_Wdt, L1_attn_bdt, L1_attn_A_log,
           L1_attn_Wout, L1_attn_bout,
           L1_ffn_gate_w, L1_ffn_w1, L1_ffn_b1, L1_ffn_w2, L1_ffn_b2,
           L1_ffn_w3, L1_ffn_b3, L1_ffn_sw1, L1_ffn_sw2, L1_ffn_sw3):
    if "f" not in _CACHE:
        _CACHE["f"] = build_fused()
    _, nc = _CACHE["f"]
    x = np.asarray(L0_attn_x, dtype=np.float32)
    B = x.shape[0]
    T = x.shape[1]
    if "wro" not in _CACHE:
        import jax
        import jax.numpy as jnp
        _cpu = jax.local_devices(backend="cpu")[0]
        with jax.default_device(_cpu):
            Dh = 64
            inv = 1.0 / 10000.0 ** (jnp.arange(0, Dh, 2, dtype=jnp.float32) / Dh)
            tt = jnp.arange(T, dtype=jnp.float32)
            fr = tt[:, None] * inv[None, :]
            _c = np.asarray(jnp.cos(fr), np.float64)
            _s = np.asarray(jnp.sin(fr), np.float64)
        w64 = _c ** 2 + _s ** 2
        whi = w64.astype(np.float32)
        wlo = (w64 - whi.astype(np.float64)).astype(np.float32)
        rows = np.arange(128) % 32
        _CACHE["wro"] = (np.ascontiguousarray(whi.T[rows, :]),
                         np.ascontiguousarray(wlo.T[rows, :]))
    wro_hi, wro_lo = _CACHE["wro"]
    base = {"Wq": _ct(L0_attn_Wq), "Wdown": _ct(L0_attn_Wdown), "Wup": _ct(L0_attn_Wup),
            "Wout0": _ct(L0_attn_Wout), "bq": _ct(L0_attn_bq), "bdown": _ct(L0_attn_bdown),
            "bup": _ct(L0_attn_bup), "bout0": _ct(L0_attn_bout),
            "wgT": _ct(L0_ffn_w_gate.T).astype(np.int8),
            "wuT": _ct(L0_ffn_w_up.T).astype(np.int8),
            "wdT": _ct(L0_ffn_w_down.T).astype(np.int8),
            "Wz": _ct(L1_attn_Wz), "Wx": _ct(L1_attn_Wx), "Wdt": _ct(L1_attn_Wdt),
            "Wout1": _ct(L1_attn_Wout), "WB": _ct(L1_attn_WB), "WC": _ct(L1_attn_WC),
            "bz": _ct(L1_attn_bz), "bx": _ct(L1_attn_bx), "bdt": _ct(L1_attn_bdt),
            "bout1": _ct(L1_attn_bout), "bB": _ct(L1_attn_bB), "bC": _ct(L1_attn_bC),
            "A_log": _ct(L1_attn_A_log), "gwT": _ct(L1_ffn_gate_w.T),
            "w1T": _ct(L1_ffn_w1.transpose(0, 2, 1)),
            "w2T": _ct(L1_ffn_w2.transpose(0, 2, 1)),
            "w3T": _ct(L1_ffn_w3.transpose(0, 2, 1)),
            "b1": _ct(L1_ffn_b1), "b2": _ct(L1_ffn_b2), "b3": _ct(L1_ffn_b3),
            "sw1T": _ct(L1_ffn_sw1.T), "sw2T": _ct(L1_ffn_sw2.T), "sw3T": _ct(L1_ffn_sw3.T),
            "Wro_hi": wro_hi, "Wro_lo": wro_lo}
    in_maps = [dict(base, xT=_ct(x[c].T)) for c in range(B)]
    res = run_bass_kernel_spmd(nc, in_maps, core_ids=list(range(B)))
    out = np.stack([np.asarray(res.results[c]["yT"]).T for c in range(B)])
    if out.dtype == np.float32 or out.dtype == np.float64:
        return out
    return out.astype(np.float32)


# version 19
# speedup vs baseline: 1.1363x; 1.0028x over previous
import contextlib
import numpy as np
import concourse.bass as bass
import concourse.mybir as mybir
from concourse.bass_utils import run_bass_kernel_spmd

F32 = mybir.dt.float32
BF16 = mybir.dt.bfloat16
I32 = mybir.dt.int32
I8 = mybir.dt.int8
AF = mybir.ActivationFunctionType
ALU = mybir.AluOpType
AX = mybir.AxisListType

D = 1024
NOPOOL = set()
KT = D // 128
NCORES = 8
ENGINES = ("tensor", "vector", "scalar", "gpsimd", "sync")
STREAMS = ENGINES


class ES:

    def __init__(self, name, dma=False):
        self.name = name
        self.step = 16 if dma else 1
        self.count = 0
        self.ops = []
        self._waited = {}

    def raw(self, f):
        self.ops.append(("op", f))

    def done(self, f):
        self.count += self.step
        self.ops.append(("inc", f))
        return (self.name, self.count)

    def tok(self):
        return (self.name, self.count)

    def wait(self, token):
        if token is None:
            return
        src, val = token
        if val == -1:
            if self._waited.get(src, 0) != -1:
                self._waited[src] = -1
                self.ops.append(("waitfull", src))
            return
        if val <= 0 or (src == self.name and val >= self.count + self.step):
            assert not (src == self.name and val > self.count), "self-wait on future token"
        if val <= self._waited.get(src, 0) and self._waited.get(src, 0) != -1:
            return
        if self._waited.get(src, 0) != -1:
            self._waited[src] = val
        self.ops.append(("wait", src, val))

    def wait_all(self, tokens):
        for t in tokens:
            self.wait(t)


class PsumPool:

    def __init__(self, banks):
        self.banks = banks
        self.last_read = [None] * len(banks)
        self.rr = 0

    def acquire(self):
        i = self.rr
        self.rr = (self.rr + 1) % len(self.banks)
        return i, self.banks[i], self.last_read[i]

    def acquire_n(self, n):
        out = [self.acquire() for _ in range(n)]
        return out

    def release(self, i, consumer_token):
        self.last_read[i] = consumer_token


class Ctx:
    def __init__(self, nc, T, TC, SUB=128):
        self.nc = nc
        self.T = T
        self.TC = TC
        self.SUB = SUB
        self.NCH = T // TC
        self.es = contextlib.ExitStack()
        self.streams = {s: ES(s) for s in STREAMS}
        self.sb_bytes = 0
        self._name_i = 0

    def channel(self, base):
        name = self.name("ch_" + base)
        ch = ES(name, dma=True)
        self.streams[name] = ch
        return ch

    @property
    def Te(self): return self.streams["tensor"]
    @property
    def V(self): return self.streams["vector"]
    @property
    def S(self): return self.streams["scalar"]
    @property
    def G(self): return self.streams["gpsimd"]
    @property
    def Y(self): return self.streams["sync"]
    def name(self, base):
        self._name_i += 1
        return f"{base}_{self._name_i}"

    def sbuf(self, base, shape, dtype):
        per_part = int(np.prod(shape[1:])) if len(shape) > 1 else 1
        nbytes = per_part * (2 if dtype == BF16 else 4)
        self.sb_bytes += 128 * nbytes
        if getattr(self, "_scr_on", False) and base not in NOPOOL:
            assert len(shape) == 2, f"scratch tensors must be 2D: {base} {shape}"
            return self.scr(dtype, shape[0], shape[1], base=0)
        return self.es.enter_context(self.nc.sbuf_tensor(self.name(base), list(shape), dtype))

    def scratch_init(self, pb_elems, pf_elems):
        self._poolb = self.es.enter_context(
            self.nc.sbuf_tensor(self.name("poolb"), [128, pb_elems], BF16))
        self._poolf = self.es.enter_context(
            self.nc.sbuf_tensor(self.name("poolf"), [128, pf_elems], F32))
        self._pool_sz = {BF16: pb_elems, F32: pf_elems}
        self._scr_on = False

    def scratch_begin(self):
        self._pool_off = {BF16: 0, F32: 0}
        self._shares = {}
        self._scr_on = True

    def scratch_end(self):
        self._scr_on = False

    def scr(self, dtype, nparts, elems, base=0, share=None, share_elems=None, share_off=0):
        pool = self._poolb if dtype == BF16 else self._poolf
        if share is not None:
            if share not in self._shares:
                off = (self._pool_off[dtype] + 15) // 16 * 16
                self._pool_off[dtype] = off + share_elems
                self._shares[share] = off
            off = self._shares[share] + share_off
        else:
            off = (self._pool_off[dtype] + 15) // 16 * 16
            self._pool_off[dtype] = off + elems
        assert self._pool_off[dtype] <= self._pool_sz[dtype], \
            f"pool overflow dtype={dtype} need {self._pool_off[dtype]} > {self._pool_sz[dtype]}"
        return pool[base:base + nparts, off:off + elems]

    def psum(self, base, shape, dtype=F32):
        return self.es.enter_context(self.nc.psum_tensor(self.name(base), list(shape), dtype))

    def hdma(self, ch, out, in_, wait=()):
        self.Y.wait_all(wait)
        ch.count += 16
        val = ch.count
        self.Y.ops.append(("dma", lambda eng, o=out, i=in_: eng.dma_start(out=o, in_=i), ch.name))
        return (ch.name, val)

    def gdma(self, ch, out, in_, wait=(), slow=False):
        self.G.wait_all(wait)
        ch.count += 16
        val = ch.count
        self.G.ops.append(("dma", lambda eng, o=out, i=in_, s=slow:
                           eng.dma_start(out=o, in_=i, allow_slow_non_contiguous=s), ch.name))
        return (ch.name, val)

    def barrier(self):
        toks = [s.tok() for s in self.streams.values()]
        for e in ENGINES:
            self.streams[e].wait_all(toks)

    def build(self, block):
        sems = {s: self.es.enter_context(self.nc.semaphore(self.name("sem_" + s)))
                for s in self.streams}

        def runner(stream):
            def _(eng):
                for op in stream.ops:
                    if op[0] == "op":
                        op[1](eng)
                    elif op[0] == "inc":
                        ins = op[1](eng)
                        ins.then_inc(sems[stream.name], stream.step)
                    elif op[0] == "wait":
                        eng.wait_ge(sems[op[1]], op[2])
                    elif op[0] == "waitfull":
                        eng.wait_ge(sems[op[1]], self.streams[op[1]].count)
                    elif op[0] == "dma":
                        op[1](eng).then_inc(sems[op[2]], 16)
            return _
        block.tensor(runner(self.streams["tensor"]))
        block.vector(runner(self.streams["vector"]))
        block.scalar(runner(self.streams["scalar"]))
        block.gpsimd(runner(self.streams["gpsimd"]))
        block.sync(runner(self.streams["sync"]))



class Common:
    def __init__(self, ctx):
        self.setup_ch = ctx.channel("setup")
        self.setup_tok = (self.setup_ch.name, -1)
        self.setupg_ch = ctx.channel("setupg")
        self.setupg_tok = (self.setupg_ch.name, -1)
        self.idf = ctx.sbuf("identf", [128, 128], F32)
        t = ctx.G.done(lambda eng, s=self.idf: eng.memset(s[:], 0.0))
        ctx.G.wait(t)
        self.idf_tok = ctx.G.done(lambda eng, s=self.idf: eng.affine_select(
            out=s[:], in_=s[:], compare_op=ALU.not_equal, fill=1.0,
            base=0, pattern=[[-1, 128]], channel_multiplier=1))
        self.idb = ctx.sbuf("identb", [128, 128], BF16)
        tb = ctx.G.done(lambda eng, s=self.idb: eng.memset(s[:], 0.0))
        ctx.G.wait(tb)
        self.idb_tok = ctx.G.done(lambda eng, s=self.idb: eng.affine_select(
            out=s[:], in_=s[:], compare_op=ALU.not_equal, fill=1.0,
            base=0, pattern=[[-1, 128]], channel_multiplier=1))
        self.onescol = ctx.sbuf("onescol", [128, 1], F32)
        self.onesrow = ctx.sbuf("onesrow", [1, 512], F32)
        self.ones16 = ctx.sbuf("ones16", [16, 1], BF16)
        self.onesrowb = ctx.sbuf("onesrowb", [1, 512], BF16)
        t1 = ctx.V.done(lambda eng, s=self.onescol: eng.memset(s[:], 1.0))
        t2 = ctx.V.done(lambda eng, s=self.onesrow: eng.memset(s[:], 1.0))
        t3 = ctx.V.done(lambda eng, s=self.ones16: eng.memset(s[:], 1.0))
        t4 = ctx.V.done(lambda eng, s=self.onesrowb: eng.memset(s[:], 1.0))
        self.ones_tok = t4
        self.ctx = ctx
        self._consts = {}
        self._pending_setup = True

    def c(self, val, p=128):
        key = float(val)
        if key not in self._consts:
            assert self._pending_setup, "register consts before finish_setup"
            cell = self.ctx.sbuf("cc", [128, 1], F32)
            self.ctx.V.done(lambda eng, cell=cell, v=key: eng.memset(cell[:], v))
            self._consts[key] = cell
        return self._consts[key][0:p, 0:1]

    def crecip(self, denom):
        key = ("recip", float(denom))
        if key not in self._consts:
            assert self._pending_setup
            cell = self.ctx.sbuf("cr", [128, 1], F32)
            t = self.ctx.V.done(lambda eng, cell=cell, v=float(denom): eng.memset(cell[:], v))
            self.ctx.V.wait(t)
            self.ctx.V.done(lambda eng, cell=cell: eng.reciprocal(cell[:], cell[:]))
            self._consts[key] = cell
        return self._consts[key]

    def finish_setup(self):
        self._pending_setup = False
        self.ctx.barrier()


def load_bias_cols(ctx, pp, com, bdram, n, name="b", shaped=False):
    kt = n // 128
    if not hasattr(com, "browbuf"):
        com.browbuf = ctx.sbuf("browbuf", [16, 128], F32)
        com.brow_war = None
        com.bias_ch = ctx.channel("biasch")
    row = com.browbuf[0:kt, :]
    t_ld = ctx.hdma(com.bias_ch, row,
                    bdram if shaped else bdram.rearrange("(a b) -> a b", b=128),
                    wait=[com.brow_war])
    bi, ps, war = pp.acquire()
    ctx.Te.wait_all([t_ld, war, com.idf_tok])
    t_tr = ctx.Te.done(lambda eng, row=row, ps=ps, com=com, kt=kt:
                       eng.transpose(ps[0:128, 0:kt], row, com.idf[0:kt, 0:kt]))
    com.brow_war = t_tr
    out = ctx.sbuf(name, [128, kt], F32)
    ctx.V.wait(t_tr)
    t_ev = ctx.V.done(lambda eng, out=out, ps=ps, kt=kt: eng.tensor_copy(out[:], ps[0:128, 0:kt]))
    pp.release(bi, t_ev)
    return out, t_ev


def load_bias_row(ctx, com, bdram, n, name="brow", dtype=F32):
    out = ctx.sbuf(name, [1, n], dtype)
    src_ap = bdram.rearrange("(a b) -> a b", b=n) if bdram.ndim == 1 else bdram
    if dtype == F32:
        ctx.hdma(com.setup_ch, out[:], src_ap)
        return out, com.setup_tok
    ctx.gdma(com.setupg_ch, out[:], src_ap)
    return out, com.setupg_tok



class WResident:

    def __init__(self, ctx, wdram, din, dout, name="w", sbuf=None, ch=None, war=None):
        self.kt = din // 128
        self.dout = dout
        self.sb = sbuf if sbuf is not None else ctx.sbuf(name, [128, self.kt * dout], BF16)
        self.ch = ch if ch is not None else ctx.channel(name)
        if self.kt == 1:
            self.tok = ctx.gdma(self.ch, self.sb[:, 0:dout], wdram[:, :],
                                wait=[war] if war is not None else ())
        else:
            self.tok = ctx.gdma(
                self.ch,
                self.sb[:, 0:self.kt * dout].rearrange("p (a n) -> p a n", a=self.kt),
                wdram.rearrange("(a p) n -> p a n", p=128),
                wait=[war] if war is not None else ())

    def view(self, ki, mo, mw=128):
        return self.sb[:, ki * self.dout + mo: ki * self.dout + mo + mw]



class KRowStream:

    def __init__(self, ctx, name, dout, krows=128):
        self.ctx = ctx
        self.dout = dout
        self.krows = krows
        self.slots = [ctx.sbuf(name + "s", [krows, dout], BF16) for _ in range(2)]
        self.chans = [ctx.channel(name + "0"), ctx.channel(name + "1")]
        self.war = [None, None]

    def load(self, wdram, ki):
        s = ki % 2
        tok = self.ctx.gdma(self.chans[s], self.slots[s][:],
                            wdram[ki * self.krows:(ki + 1) * self.krows, :],
                            wait=[self.war[s]])
        return self.slots[s], tok

    def used(self, ki, mm_tok):
        self.war[ki % 2] = mm_tok


def emit_linear_krows(ctx, pp, stream, wdram, din, dout, rhs_aps, rhs_tok, evac,
                      krows=128, bias_rank1=None, extra_k=None):
    kt = din // krows
    mt = dout // 128
    assert mt <= len(pp.banks)
    banks = pp.acquire_n(mt)
    ctx.Te.wait_all([w for (_, _, w) in banks])
    ctx.Te.wait(rhs_tok)
    n = rhs_aps[0].shape[-1]
    n_tail = (1 if bias_rank1 is not None else 0) + (len(extra_k) if extra_k else 0)
    mm_tok = None
    for ki in range(kt):
        slot, ltok = stream.load(wdram, ki)
        ctx.Te.wait(ltok)
        for mi in range(mt):
            _, ps, _ = banks[mi]
            stop = (ki == kt - 1) and n_tail == 0
            f = (lambda eng, ps=ps, slot=slot, mi=mi, rhs=rhs_aps[ki], ki=ki, stop=stop, n=n:
                 eng.matmul(ps[:, 0:n], slot[:, mi * 128:(mi + 1) * 128], rhs,
                            start=(ki == 0), stop=stop))
            if mi == mt - 1:
                mm_tok = ctx.Te.done(f)
                stream.used(ki, mm_tok)
            else:
                ctx.Te.raw(f)
    tails = []
    if bias_rank1 is not None:
        lf, rrow, btok = bias_rank1
        tails.append((lf, rrow, btok))
    if extra_k:
        tails.extend(extra_k)
    for j, (lf, r_ap, tok2) in enumerate(tails):
        ctx.Te.wait(tok2)
        for mi in range(mt):
            _, ps, _ = banks[mi]
            stop = (j == len(tails) - 1)
            f = (lambda eng, ps=ps, l=lf(mi), r=r_ap, n=n, stop=stop:
                 eng.matmul(ps[:, 0:n], l, r[0:1, 0:n], start=False, stop=stop))
            mm_tok = ctx.Te.done(f) if (stop and mi == mt - 1) else ctx.Te.raw(f)
    ev_tok = evac([(banks[mi][0], banks[mi][1], mi) for mi in range(mt)], mm_tok)
    for mi in range(mt):
        pp.release(banks[mi][0], ev_tok)
    return ev_tok


def emit_linear_resident(ctx, pp, w, rhs_aps, rhs_tok, evac, n=None, extra_k=None):
    kt = w.kt
    mt = max(1, w.dout // 128)
    if n is None:
        n = rhs_aps[0].shape[-1]
    ctx.Te.wait_all([w.tok, rhs_tok])
    ntail = len(extra_k) if extra_k else 0
    last = None
    for mi in range(mt):
        mw = min(128, w.dout - mi * 128)
        bi, ps, war = pp.acquire()
        ctx.Te.wait(war)
        for ki in range(kt):
            stop = (ki == kt - 1) and ntail == 0
            f = (lambda eng, ps=ps, w=w, ki=ki, mi=mi, mw=mw, rhs=rhs_aps[ki], n=n, stop=stop:
                 eng.matmul(ps[0:mw, 0:n], w.view(ki, mi * 128, mw), rhs,
                            start=(ki == 0), stop=stop))
            mm_tok = ctx.Te.done(f) if (ki == kt - 1 and ntail == 0) else ctx.Te.raw(f)
        if extra_k:
            for j, (lf, r_ap, tok2) in enumerate(extra_k):
                ctx.Te.wait(tok2)
                stop = (j == ntail - 1)
                f = (lambda eng, ps=ps, l=lf(mi), r=r_ap, n=n, mw=mw, stop=stop:
                     eng.matmul(ps[0:mw, 0:n], l[:, 0:mw], r[0:1, 0:n], start=False, stop=stop))
                mm_tok = ctx.Te.done(f) if stop else ctx.Te.raw(f)
        ev = evac(bi, ps, mi, mw, mm_tok)
        pp.release(bi, ev)
        last = ev
    return last



class LNState:
    def __init__(self, ctx):
        self.sq = ctx.sbuf("lnsq", [128, 2 * ctx.TC], F32)
        self.sq_war = None
        self.rows = ctx.sbuf("lnrows", [1, 5 * ctx.TC], F32)
        self.t1 = ctx.sbuf("lnt1", [128, ctx.TC], F32)
        self.t1_war = None

    def row(self, j, ctx):
        return self.rows[0:1, j * ctx.TC:(j + 1) * ctx.TC]


def emit_ln_chunk(ctx, pp, com, ln, R, r_tok, c, xn, xn_war=None):
    TC = ctx.TC
    cs = slice(c * TC, (c + 1) * TC)
    bi_s, ps_s, war_s = pp.acquire()
    bi_q, ps_q, war_q = pp.acquire()
    ctx.Te.wait_all([war_s, war_q, r_tok])
    ctx.S.wait(r_tok)
    t_mm = None
    for half in range(4):
        ctx.S.wait(ln.sq_war)
        for i4 in range(2):
            i = half * 2 + i4
            f = (lambda eng, ln=ln, R=R, i=i, i4=i4, cs=cs, TC=TC:
                 eng.activation(ln.sq[:, i4 * TC:(i4 + 1) * TC], R[i][:, cs], AF.Square))
            t_sq = ctx.S.done(f) if i4 == 1 else ctx.S.raw(f)
        ctx.Te.wait(t_sq)
        for i4 in range(2):
            i = half * 2 + i4
            ctx.Te.raw(lambda eng, ps=ps_s, com=com, R=R, i=i, cs=cs:
                       eng.matmul(ps[0:1, :], com.onescol[:], R[i][:, cs],
                                  start=(i == 0), stop=(i == KT - 1)))
            f = (lambda eng, ps=ps_q, com=com, ln=ln, i4=i4, i=i, TC=TC:
                 eng.matmul(ps[0:1, :], com.onescol[:], ln.sq[:, i4 * TC:(i4 + 1) * TC],
                            start=(i == 0), stop=(i == KT - 1)))
            t_mm = ctx.Te.done(f) if i4 == 1 else ctx.Te.raw(f)
        ln.sq_war = t_mm
    mean, var, rstd, mr = (ln.row(j, ctx) for j in range(4))
    tmp = ln.row(4, ctx)
    ctx.V.wait(t_mm)
    ctx.V.raw(lambda eng, mean=mean, ps=ps_s: eng.tensor_scalar_mul(mean, ps[0:1, :], 1.0 / D))
    t_v0 = ctx.V.done(lambda eng, var=var, ps=ps_q: eng.tensor_scalar_mul(var, ps[0:1, :], 1.0 / D))
    ctx.V.wait(t_v0)
    t_tm = ctx.V.done(lambda eng, tmp=tmp, mean=mean: eng.tensor_mul(tmp, mean, mean))
    ctx.V.wait(t_tm)
    t_v1 = ctx.V.done(lambda eng, var=var, tmp=tmp: eng.tensor_sub(var, var, tmp))
    ctx.S.wait(t_v1)
    _eps = com.c(1e-5, p=1)
    _nh = com.c(-0.5, p=1)
    t_s1 = ctx.S.done(lambda eng, tmp=tmp, var=var, b=_eps: eng.activation(tmp, var, AF.Ln, bias=b))
    ctx.S.wait(t_s1)
    t_s2 = ctx.S.done(lambda eng, rstd=rstd, tmp=tmp, s=_nh: eng.activation(rstd, tmp, AF.Exp, scale=s))
    ctx.V.wait(t_s2)
    t_v2 = ctx.V.done(lambda eng, mr=mr, mean=mean, rstd=rstd: eng.tensor_mul(mr, mean, rstd))
    pp.release(bi_s, t_v0)
    pp.release(bi_q, t_v0)
    bi_r, ps_r, war_r = pp.acquire()
    bi_m, ps_m, war_m = pp.acquire()
    ctx.Te.wait_all([war_r, war_m, t_v2])
    t_br = ctx.Te.done(lambda eng, ps=ps_r, com=com, rstd=rstd:
                       eng.matmul(ps[:, :], com.onesrow[0:1, 0:128], rstd, start=True, stop=True))
    t_bm = ctx.Te.done(lambda eng, ps=ps_m, com=com, mr=mr:
                       eng.matmul(ps[:, :], com.onesrow[0:1, 0:128], mr, start=True, stop=True))
    ctx.V.wait_all([t_br, t_bm, xn_war])
    last = None
    for i in range(KT):
        ctx.V.wait(ln.t1_war)
        t_m = ctx.V.done(lambda eng, ln=ln, R=R, i=i, cs=cs, ps=ps_r:
                         eng.tensor_mul(ln.t1[:], R[i][:, cs], ps[:, :]))
        ctx.V.wait(t_m)
        last = ctx.V.done(lambda eng, ln=ln, xn=xn, i=i, ps=ps_m:
                          eng.tensor_sub(xn[i], ln.t1[:], ps[:, :]))
        ln.t1_war = last
    pp.release(bi_r, last)
    pp.release(bi_m, last)
    return last



class MGroupStream:

    def __init__(self, ctx, name, din, mgw, nslots=2):
        self.ctx = ctx
        self.kt = din // 128
        self.mgw = mgw
        self.nslots = nslots
        self.s1 = [ctx.sbuf(name + "a", [128, self.kt * mgw], BF16) for _ in range(nslots)]
        self.s2 = [ctx.sbuf(name + "b", [128, self.kt * mgw], BF16) for _ in range(nslots)]
        self.chans = [ctx.channel(name + str(p)) for p in range(nslots)]
        self.war = [None] * nslots

    def load(self, w1d, w2d, mg):
        p = mg % self.nslots
        tok = None
        for sb, wd in ((self.s1[p], w1d), (self.s2[p], w2d)):
            tok = self.ctx.gdma(
                self.chans[p],
                sb[:, 0:self.kt * self.mgw].rearrange("p (a n) -> p a n", a=self.kt),
                wd[:, mg * self.mgw:(mg + 1) * self.mgw].rearrange(
                    "(a p) n -> p a n", p=128),
                wait=[self.war[p]] if tok is None else ())
        return tok

    def view(self, which, ki, msub, mg=0):
        sb = (self.s1 if which == 0 else self.s2)[mg % self.nslots]
        return sb[:, ki * self.mgw + msub * 128: ki * self.mgw + msub * 128 + 128]


class SwigluBufs:
    def __init__(self, ctx, H, mgw, w3s=None, nslots=2):
        self.H = H
        self.mgw = mgw
        self.hbuf = ctx.sbuf("hbuf", [128, (H // 128) * ctx.TC], BF16)
        self.gt = [ctx.sbuf("gt", [128, ctx.TC], BF16)]
        self.gt_war = [None]
        self.mgs = MGroupStream(ctx, "mgs", D, mgw, nslots=nslots)
        self.w3s = w3s if w3s is not None else KRowStream(ctx, "w3s", D)


def emit_swiglu_chunk(ctx, pp, com, sw, xn, xn_tok, w1d, w2d, w3d, s1c, s3c, R, c):
    TC = ctx.TC
    cs = slice(c * TC, (c + 1) * TC)
    H = sw.H
    nmg = H // sw.mgw
    msubs = sw.mgw // 128
    h_tok = None
    pre = sw.mgs.nslots > 1
    if pre:
        wtoks = [sw.mgs.load(w1d, w2d, 0)]
    for mg in range(nmg):
        if pre:
            if mg + 1 < nmg:
                wtoks.append(sw.mgs.load(w1d, w2d, mg + 1))
            wtok = wtoks[mg]
        else:
            wtok = sw.mgs.load(w1d, w2d, mg)
        for msub in range(msubs):
            kh = mg * msubs + msub
            bi_g, ps_g, war_g = pp.acquire()
            bi_u, ps_u, war_u = pp.acquire()
            ctx.Te.wait_all([war_g, war_u, wtok, xn_tok])
            for ki in range(KT):
                ctx.Te.raw(lambda eng, ps=ps_g, sw=sw, ki=ki, msub=msub, xn=xn, mg=mg:
                           eng.matmul(ps[:, :], sw.mgs.view(0, ki, msub, mg), xn[ki],
                                      start=(ki == 0), stop=(ki == KT - 1)))
                f = (lambda eng, ps=ps_u, sw=sw, ki=ki, msub=msub, xn=xn, mg=mg:
                     eng.matmul(ps[:, :], sw.mgs.view(1, ki, msub, mg), xn[ki],
                                start=(ki == 0), stop=(ki == KT - 1)))
                t_mm = ctx.Te.done(f) if ki == KT - 1 else ctx.Te.raw(f)
            gsel = kh % len(sw.gt)
            gt = sw.gt[gsel]
            ctx.S.wait_all([t_mm, sw.gt_war[gsel]])
            if s1c is None:
                t_g = ctx.S.done(lambda eng, gt=gt, ps=ps_g:
                                 eng.activation(gt[:], ps[:, :], AF.Silu))
            else:
                t_g = ctx.S.done(lambda eng, gt=gt, ps=ps_g, s=s1c[0:128, 0:1]:
                                 eng.activation(gt[:], ps[:, :], AF.Silu, scale=s))
            ctx.V.wait_all([t_g, t_mm])
            s1arg = 1.0 if s1c is None else s1c[0:128, 0:1]
            h_tok = ctx.V.done(lambda eng, sw=sw, kh=kh, ps=ps_u, s1=s1arg, gt=gt, TC=TC:
                               eng.scalar_tensor_tensor(
                                   sw.hbuf[:, kh * TC:(kh + 1) * TC], ps[:, :], s1, gt[:],
                                   ALU.mult, ALU.mult))
            sw.gt_war[gsel] = h_tok
            pp.release(bi_g, t_g)
            pp.release(bi_u, h_tok)
            sw.mgs.war[mg % sw.mgs.nslots] = t_mm
    rhs = [sw.hbuf[:, kh * TC:(kh + 1) * TC] for kh in range(H // 128)]

    s3arg = 1.0 if s3c is None else s3c[0:128, 0:1]

    def evac(bank_list, mm_tok):
        ctx.V.wait(mm_tok)
        last = None
        for (bi, ps, mi) in bank_list:
            last = ctx.V.done(lambda eng, ps=ps, mi=mi, R=R, cs=cs, s3=s3arg:
                              eng.scalar_tensor_tensor(R[mi][:, cs], ps[:, :], s3,
                                                       R[mi][:, cs], ALU.mult, ALU.add))
        return last

    return emit_linear_krows(ctx, pp, sw.w3s, w3d, H, D, rhs, h_tok, evac)



def build_l0f(T=2048, TC=512, H=3072, mgw=256):
    nc = bass.Bass("TRN2", target_bir_lowering=False, debug=False)
    xin = nc.dram_tensor("xT", [D, T], F32, kind="ExternalInput").ap()
    wg = nc.dram_tensor("wgT", [D, H], I32, kind="ExternalInput").ap()
    wu = nc.dram_tensor("wuT", [D, H], I32, kind="ExternalInput").ap()
    wd = nc.dram_tensor("wdT", [H, D], I32, kind="ExternalInput").ap()
    xout = nc.dram_tensor("yT", [D, T], F32, kind="ExternalOutput").ap()
    ctx = Ctx(nc, T, TC)
    with ctx.es:
        banks = [ctx.psum("bank", [128, TC]) for _ in range(8)]
        pp = PsumPool(banks)
        com = Common(ctx)
        com.c(1e-5); com.c(-0.5)
        r127 = com.crecip(127.0)
        com.finish_setup()
        ln = LNState(ctx)
        sw = SwigluBufs(ctx, H, mgw)
        R = [ctx.sbuf("R", [128, T], F32) for _ in range(KT)]
        XN = ctx.sbuf("XN", [128, KT * TC], BF16)
        xn = [XN[:, i * TC:(i + 1) * TC] for i in range(KT)]
        ch_r = ctx.channel("rload")
        r_toks = [ctx.hdma(ch_r, R[i][:], xin[i * 128:(i + 1) * 128, :]) for i in range(KT)]
        r_tok = r_toks[-1]
        for c in range(ctx.NCH):
            t_ln = emit_ln_chunk(ctx, pp, com, ln, R, r_tok, c, xn)
            t_f = emit_swiglu_chunk(ctx, pp, com, sw, xn, t_ln, wg, wu, wd,
                                    r127, r127, R, c)
            r_tok = t_f
        ctx.barrier()
        ch_o = ctx.channel("store")
        st = [ctx.hdma(ch_o, xout[i * 128:(i + 1) * 128, :], R[i][:]) for i in range(KT)]
        ctx.Y.wait(st[-1])
        with nc.Block() as block:
            ctx.build(block)
    return nc



class SSMBufs:
    def __init__(self, ctx, pb=None):
        TC = ctx.TC
        self.DT = ctx.sbuf("DT", [128, KT * TC], F32)
        self.U = ctx.sbuf("U", [128, KT * TC], BF16)
        self.Z = ctx.sbuf("Z", [128, KT * TC], BF16)
        self.YZ = ctx.sbuf("YZ", [128, KT * TC], BF16)
        self.BBH = ctx.sbuf("BBH", [128, 8 * TC], BF16)
        self.CBH = ctx.sbuf("CBH", [128, 8 * TC], BF16)
        self.HJ = ctx.sbuf("HJ", [128, 8 * TC], BF16)
        self.PB = pb if pb is not None else ctx.sbuf("PB", [128, 8 * TC], BF16)
        self.Brow = ctx.sbuf("Brow", [16, TC], BF16)
        self.Crow = ctx.sbuf("Crow", [16, TC], BF16)
        self.Brow0 = ctx.sbuf("Brow0", [1, 2 * TC], BF16)
        self.Crow0 = ctx.sbuf("Crow0", [1, 2 * TC], BF16)
        self.carry = ctx.sbuf("carry", [128, 128], F32)
        self.ab = [ctx.sbuf("ab", [128, TC], BF16) for _ in range(2)]
        self.bb = [ctx.sbuf("bb", [128, TC], BF16) for _ in range(2)]
        self.ab_war = [None] * 2
        self.bb_war = [None] * 2
        self.escr = [ctx.sbuf("escr", [128, TC], F32)]
        self.escr_war = [None]
        self.YH = ctx.sbuf("YH", [128, KT * TC], BF16)
        self.yh = [ctx.sbuf("yh", [128, TC], F32) for _ in range(2)]
        self.yh_war = [None, None]
        self.ch_bc = ctx.channel("bcrow0")
        self.bc_war = None
        self.carry_tok = None


def emit_l1a_chunk(ctx, pp, com, ln, sb, ws, xn, xn_tok, R, c,
                   wz, wx, wdt, wout, wB, wC, bz, bx, bdt, bB16, bC16, boutrow,
                   Asb, A_tok, first, post_mm=None):
    TC = ctx.TC
    cs = slice(c * TC, (c + 1) * TC)

    def evac_dt(bank_list, mm_tok):
        ctx.S.wait(mm_tok)
        last = None
        for (bi, ps, mi) in bank_list:
            es_ = mi % len(sb.escr)
            ctx.S.wait(sb.escr_war[es_])
            t_e = ctx.S.done(lambda eng, sb=sb, es_=es_, ps=ps, b=bdt, mi=mi:
                             eng.activation(sb.escr[es_][:], ps[:, :], AF.Exp, bias=b[:, mi:mi + 1]))
            ctx.S.wait(t_e)
            last = ctx.S.done(lambda eng, sb=sb, es_=es_, mi=mi, one=com.c(1.0), TC=TC:
                              eng.activation(sb.DT[:, mi * TC:(mi + 1) * TC],
                                             sb.escr[es_][:], AF.Ln, bias=one))
            sb.escr_war[es_] = last
        return last
    t_dt = emit_linear_krows(ctx, pp, ws, wdt, D, D, xn, xn_tok, evac_dt)

    def evac_u(bank_list, mm_tok):
        ctx.S.wait(mm_tok)
        last = None
        for (bi, ps, mi) in bank_list:
            es_ = mi % len(sb.escr)
            ctx.S.wait(sb.escr_war[es_])
            t_x = ctx.S.done(lambda eng, sb=sb, es_=es_, ps=ps, b=bx, mi=mi:
                             eng.activation(sb.escr[es_][:], ps[:, :], AF.Identity, bias=b[:, mi:mi + 1]))
            ctx.V.wait_all([t_x, t_dt])
            last = ctx.V.done(lambda eng, sb=sb, es_=es_, mi=mi, TC=TC:
                              eng.tensor_mul(sb.U[:, mi * TC:(mi + 1) * TC],
                                             sb.escr[es_][:], sb.DT[:, mi * TC:(mi + 1) * TC]))
            sb.escr_war[es_] = last
        return last
    t_u = emit_linear_krows(ctx, pp, ws, wx, D, D, xn, xn_tok, evac_u)

    def evac_z(bank_list, mm_tok):
        ctx.S.wait(mm_tok)
        last = None
        for (bi, ps, mi) in bank_list:
            last = ctx.S.done(lambda eng, sb=sb, ps=ps, b=bz, mi=mi, TC=TC:
                              eng.activation(sb.Z[:, mi * TC:(mi + 1) * TC], ps[:, :],
                                             AF.Sigmoid, bias=b[:, mi:mi + 1]))
        return last
    t_z = emit_linear_krows(ctx, pp, ws, wz, D, D, xn, xn_tok, evac_z)

    def evac_b(bi, ps, mi, mw, mm_tok):
        ctx.S.wait(mm_tok)
        return ctx.S.done(lambda eng, sb=sb, ps=ps, b=bB16:
                          eng.activation(sb.Brow[:], ps[0:16, :], AF.Identity, bias=b[0:16, 0:1]))
    t_B = emit_linear_resident(ctx, pp, wB, xn, xn_tok, evac_b)

    def evac_c(bi, ps, mi, mw, mm_tok):
        ctx.S.wait(mm_tok)
        return ctx.S.done(lambda eng, sb=sb, ps=ps, b=bC16:
                          eng.activation(sb.Crow[:], ps[0:16, :], AF.Identity, bias=b[0:16, 0:1]))
    t_C = emit_linear_resident(ctx, pp, wC, xn, xn_tok, evac_c)
    if post_mm is not None:
        post_mm(t_C)

    carry_new = None
    yz_tok = None
    half_guard = None
    hj_war = None
    pb_war = None
    for jh in range(2):
        t_bb = t_cc = None
        for j8 in range(8):
            if j8 % 2 == 0:
                r0 = jh * 8 + j8
                t_B0 = ctx.gdma(sb.ch_bc, sb.Brow0[:], sb.Brow[r0:r0 + 2, :],
                                wait=[t_B, t_C, sb.bc_war])
                t_C0 = ctx.gdma(sb.ch_bc, sb.Crow0[:], sb.Crow[r0:r0 + 2, :])
            jq = j8 % 2
            for (dst, src0) in ((sb.BBH, sb.Brow0), (sb.CBH, sb.Crow0)):
                bi, ps, war = pp.acquire()
                ctx.Te.wait_all([war, t_C0, com.ones_tok])
                t_mm = ctx.Te.done(lambda eng, ps=ps, com=com, src0=src0, jq=jq, TC=TC:
                                   eng.matmul(ps[:, :], com.onesrowb[0:1, 0:128],
                                              src0[0:1, jq * TC:(jq + 1) * TC],
                                              start=True, stop=True))
                ctx.S.wait_all([t_mm, half_guard])
                t_ev = ctx.S.done(lambda eng, dst=dst, j8=j8, ps=ps, TC=TC:
                                  eng.activation(dst[:, j8 * TC:(j8 + 1) * TC], ps[:, :], AF.Copy))
                pp.release(bi, t_ev)
                if dst is sb.BBH:
                    t_bb = t_ev
                else:
                    t_cc = t_ev
            if j8 % 2 == 1:
                sb.bc_war = t_mm
        t_y = None
        for i in range(KT):
            scan_toks = []
            for j8 in range(8):
                j = jh * 8 + j8
                a_ = (i * 8 + j8) % len(sb.ab)
                ctx.S.wait_all([sb.ab_war[a_], t_dt, A_tok])
                t_ab = ctx.S.done(lambda eng, sb=sb, a_=a_, i=i, j=j, Asb=Asb, TC=TC:
                                  eng.activation(sb.ab[a_][:], sb.DT[:, i * TC:(i + 1) * TC],
                                                 AF.Exp, scale=Asb[:, i * 16 + j:i * 16 + j + 1]))
                ctx.V.wait_all([sb.bb_war[a_], t_u, t_bb])
                t_b = ctx.V.done(lambda eng, sb=sb, a_=a_, i=i, j8=j8, TC=TC:
                                 eng.tensor_mul(sb.bb[a_][:], sb.U[:, i * TC:(i + 1) * TC],
                                                sb.BBH[:, j8 * TC:(j8 + 1) * TC]))
                ctx.V.wait_all([t_ab, t_b, sb.carry_tok, hj_war])
                init = 0.0 if first else sb.carry[:, i * 16 + j:i * 16 + j + 1]
                t_sc = ctx.V.done(lambda eng, sb=sb, a_=a_, j8=j8, init=init, TC=TC:
                                  eng.tensor_tensor_scan(sb.HJ[:, j8 * TC:(j8 + 1) * TC],
                                                         sb.ab[a_][:], sb.bb[a_][:], init,
                                                         ALU.mult, ALU.add))
                scan_toks.append(t_sc)
                sb.ab_war[a_] = t_sc
                sb.bb_war[a_] = t_sc
            ctx.V.wait(scan_toks[-1])
            carry_new = ctx.V.done(lambda eng, sb=sb, i=i, jh=jh, TC=TC:
                                   eng.tensor_copy(
                                       sb.carry[:, i * 16 + jh * 8:i * 16 + jh * 8 + 8],
                                       sb.HJ[:, TC - 1::TC]))
            ctx.V.wait_all([t_cc, pb_war])
            t_p = ctx.V.done(lambda eng, sb=sb: eng.tensor_mul(sb.PB[:], sb.HJ[:], sb.CBH[:]))
            hj_war = t_p
            ctx.V.wait(t_p)
            if jh == 0:
                def _red(eng, sb=sb, i=i, TC=TC, ctx=ctx):
                    with ctx.nc.allow_low_precision("bf16 half-sum output; DVE reduce is fp32 internal"):
                        return eng.reduce_sum(sb.YH[:, i * TC:(i + 1) * TC],
                                              sb.PB[:].rearrange("p (a b) -> p b a", b=TC),
                                              axis=AX.X)
                t_y = ctx.V.done(_red)
            else:
                ysel = i % 2
                ctx.V.wait(sb.yh_war[ysel])
                t_y = ctx.V.done(lambda eng, sb=sb, ysel=ysel, TC=TC:
                                 eng.reduce_sum(sb.yh[ysel][:],
                                                sb.PB[:].rearrange("p (a b) -> p b a", b=TC),
                                                axis=AX.X))
                ctx.V.wait(t_y)
                t_ya = ctx.V.done(lambda eng, sb=sb, ysel=ysel, i=i, TC=TC:
                                  eng.tensor_add(sb.yh[ysel][:], sb.yh[ysel][:],
                                                 sb.YH[:, i * TC:(i + 1) * TC]))
                ctx.V.wait_all([t_ya, t_z])
                yz_tok = ctx.V.done(lambda eng, sb=sb, ysel=ysel, i=i, TC=TC:
                                    eng.tensor_mul(sb.YZ[:, i * TC:(i + 1) * TC],
                                                   sb.yh[ysel][:], sb.Z[:, i * TC:(i + 1) * TC]))
                sb.yh_war[ysel] = yz_tok
            pb_war = t_y
        half_guard = t_y
    sb.carry_tok = carry_new

    rhs = [sb.YZ[:, i * TC:(i + 1) * TC] for i in range(KT)]

    def evac_o(bank_list, mm_tok):
        ctx.V.wait(mm_tok)
        last = None
        for (bi, ps, mi) in bank_list:
            last = ctx.V.done(lambda eng, ps=ps, mi=mi, R=R, cs=cs:
                              eng.scalar_tensor_tensor(R[mi][:, cs], ps[:, :], 0.0,
                                                       R[mi][:, cs], ALU.bypass, ALU.add))
        return last
    t_o = emit_linear_krows(ctx, pp, ws, wout, D, D, rhs, yz_tok, evac_o,
                            bias_rank1=(lambda mi: boutrow[0:1, mi * 128:mi * 128 + 128],
                                        com.onesrowb, com.ones_tok))
    return t_o


def build_l1a(T=2048, TC=512):
    nc = bass.Bass("TRN2", target_bir_lowering=False, debug=False)
    xin = nc.dram_tensor("xT", [D, T], F32, kind="ExternalInput").ap()
    dws = {}
    for nm, dout in (("Wz", D), ("Wx", D), ("Wdt", D), ("Wout", D), ("WB", 16), ("WC", 16)):
        dws[nm] = nc.dram_tensor(nm, [D, dout], F32, kind="ExternalInput").ap()
    dbs = {}
    for nm, n in (("bz", D), ("bx", D), ("bdt", D), ("bout", D), ("bB", 16), ("bC", 16)):
        dbs[nm] = nc.dram_tensor(nm, [n], F32, kind="ExternalInput").ap()
    alog = nc.dram_tensor("A_log", [D, 16], F32, kind="ExternalInput").ap()
    xout = nc.dram_tensor("yT", [D, T], F32, kind="ExternalOutput").ap()
    ctx = Ctx(nc, T, TC)
    with ctx.es:
        banks = [ctx.psum("bank", [128, TC]) for _ in range(8)]
        pp = PsumPool(banks)
        com = Common(ctx)
        com.c(1.0); com.c(1e-5); com.c(-0.5)
        com.finish_setup()
        ln = LNState(ctx)
        XN = ctx.sbuf("XN", [128, KT * TC], BF16)
        sb = SSMBufs(ctx, pb=XN)
        ws = KRowStream(ctx, "ws", D)
        wB = WResident(ctx, dws["WB"], D, 16, name="wB")
        wC = WResident(ctx, dws["WC"], D, 16, name="wC")
        bz, _ = load_bias_cols(ctx, pp, com, dbs["bz"], D, name="bz")
        bx, _ = load_bias_cols(ctx, pp, com, dbs["bx"], D, name="bx")
        bdt, _ = load_bias_cols(ctx, pp, com, dbs["bdt"], D, name="bdt")
        boutrow, _ = load_bias_row(ctx, com, dbs["bout"], D, name="bout", dtype=BF16)
        bB16 = ctx.sbuf("bB16", [16, 1], F32)
        bC16 = ctx.sbuf("bC16", [16, 1], F32)
        ctx.hdma(com.setup_ch, bB16[:], dbs["bB"].rearrange("(a b) -> a b", b=1))
        ctx.hdma(com.setup_ch, bC16[:], dbs["bC"].rearrange("(a b) -> a b", b=1))
        Araw = ctx.sbuf("Araw", [128, 128], F32)
        for ii in range(KT):
            ctx.hdma(com.setup_ch, Araw[:, ii * 16:(ii + 1) * 16],
                     alog[ii * 128:(ii + 1) * 128, :])
        ta = com.setup_tok
        Asb = ctx.sbuf("Asb", [128, 128], F32)
        ctx.S.wait(ta)
        tae = ctx.S.done(lambda eng: eng.activation(Asb[:], Araw[:], AF.Exp))
        ctx.V.wait(tae)
        A_tok = ctx.V.done(lambda eng: eng.tensor_scalar_mul(Asb[:], Asb[:], -1.0))
        R = [ctx.sbuf("R", [128, T], F32) for _ in range(KT)]
        xn = [XN[:, i * TC:(i + 1) * TC] for i in range(KT)]
        ch_r = ctx.channel("rload")
        r_toks = [ctx.hdma(ch_r, R[i][:], xin[i * 128:(i + 1) * 128, :]) for i in range(KT)]
        r_tok = r_toks[-1]
        ctx.barrier()
        for c in range(ctx.NCH):
            t_ln = emit_ln_chunk(ctx, pp, com, ln, R, r_tok, c, xn)
            r_tok = emit_l1a_chunk(ctx, pp, com, ln, sb, ws, xn, t_ln, R, c,
                                   dws["Wz"], dws["Wx"], dws["Wdt"], dws["Wout"],
                                   wB, wC, bz, bx, bdt, bB16, bC16, boutrow,
                                   Asb, A_tok, first=(c == 0))
        ctx.barrier()
        ch_o = ctx.channel("store")
        st = [ctx.hdma(ch_o, xout[i * 128:(i + 1) * 128, :], R[i][:]) for i in range(KT)]
        ctx.Y.wait(st[-1])
        with nc.Block() as block:
            ctx.build(block)
    return nc



class AttnBufs:
    def __init__(self, ctx):
        TC, SUB = ctx.TC, ctx.SUB
        self.Q = ctx.sbuf("Qb", [128, KT * TC], BF16)
        self.KV = ctx.sbuf("KVb", [128, 16 * TC], BF16)
        self.LAT = ctx.sbuf("LAT", [128, 2 * TC], BF16)
        if getattr(ctx, "_scr_on", False):
            self.vflat = ctx.scr(BF16, 16, 64 * SUB, base=0, share="att", share_elems=64 * SUB)
            self.qbig = [ctx.scr(BF16, 64, 16 * SUB, base=64, share="att", share_off=j * 16 * SUB)
                         for j in (0, 2)]
            self.kbig = [ctx.scr(BF16, 64, 16 * SUB, base=64, share="att", share_off=j * 16 * SUB)
                         for j in (1, 3)]
        else:
            self.qbig = [ctx.sbuf("qbig", [64, 16 * SUB], BF16) for _ in range(2)]
            self.kbig = [ctx.sbuf("kbig", [64, 16 * SUB], BF16) for _ in range(2)]
            self.vflat = ctx.sbuf("vflat", [16, 64 * SUB], BF16)
        self.Ebuf = [ctx.sbuf("Ebuf", [16, 16 * SUB], BF16) for _ in range(2)]
        self.AOB = ctx.sbuf("AOB", [64, 16 * TC], BF16)
        self.Zr = ctx.sbuf("Zr", [1, 16 * SUB], F32)
        self.ch_q = [ctx.channel("chq0"), ctx.channel("chq1")]
        self.ch_k = [ctx.channel("chk0"), ctx.channel("chk1")]
        self.ch_v = ctx.channel("chv")
        self.wro = None
        self.qbig_war = [None, None]
        self.kbig_war = [None, None]
        self.vflat_war = None
        self.ebuf_war = [None, None]
        self.zr_war = None
        self.aob_war = None


def emit_l0a_chunk(ctx, pp, com, ab, ws, ws64, xn, xn_tok, R, c,
                   wqd, wdown, wup, woutd, bq, bdn, bup, boutrow):
    TC, SUB = ctx.TC, ctx.SUB
    cs = slice(c * TC, (c + 1) * TC)
    nsub = TC // SUB

    def evac_q(bank_list, mm_tok):
        ctx.S.wait(mm_tok)
        last = None
        for (bi, ps, mi) in bank_list:
            last = ctx.S.done(lambda eng, ab=ab, ps=ps, mi=mi, b=bq, TC=TC:
                              eng.activation(ab.Q[:, mi * TC:(mi + 1) * TC], ps[:, :],
                                             AF.Identity, bias=b[:, mi:mi + 1]))
        return last
    t_q = emit_linear_krows(ctx, pp, ws, wqd, D, D, xn, xn_tok, evac_q)
    if getattr(ab, "wro", None) is not None:
        ctx.V.wait_all([t_q, ab.wro_tok])
        for mi in range(KT):
            t_q = ctx.V.done(lambda eng, ab=ab, mi=mi, c=c, TC=TC:
                             eng.tensor_mul(ab.Q[:, mi * TC:(mi + 1) * TC],
                                            ab.Q[:, mi * TC:(mi + 1) * TC],
                                            ab.wro[0:128, c * TC:(c + 1) * TC]))

    def evac_l(bi, ps, mi, mw, mm_tok):
        ctx.S.wait(mm_tok)
        return ctx.S.done(lambda eng, ab=ab, ps=ps, mi=mi, b=bdn, TC=TC:
                          eng.activation(ab.LAT[:, mi * TC:(mi + 1) * TC], ps[:, :],
                                         AF.Identity, bias=b[:, mi:mi + 1]))
    t_l = emit_linear_resident(ctx, pp, wdown, xn, xn_tok, evac_l)

    lat_aps = [ab.LAT[:, 0 * TC:(0 + 1) * TC], ab.LAT[:, 1 * TC:(1 + 1) * TC]]
    def evac_kv(bi, ps, mi, mw, mm_tok):
        ctx.S.wait(mm_tok)
        return ctx.S.done(lambda eng, ab=ab, ps=ps, mi=mi, b=bup, TC=TC:
                          eng.activation(ab.KV[:, mi * TC:(mi + 1) * TC], ps[:, :],
                                         AF.Identity, bias=b[:, mi:mi + 1]))
    t_kv = emit_linear_resident(ctx, pp, wup, lat_aps, t_l, evac_kv)

    c0125 = com.c(0.125, p=16)
    aob_tok = None

    def rebuild(sub):
        par = sub % 2
        s0 = sub * SUB
        qb4 = ab.qbig[par][:, :].rearrange("p (j q t) -> p j q t", q=2, t=SUB)
        t_rq = None
        for pq in range(2):
            t_rq = ctx.gdma(ab.ch_q[par], qb4[:, :, pq, :],
                            ab.Q[pq * 64:pq * 64 + 64, :].rearrange(
                                "p (j t) -> p j t", t=TC)[:, :, s0:s0 + SUB],
                            wait=[ab.qbig_war[par], t_q] if pq == 0 else ())
        kv3 = ab.KV[0:64, :].rearrange("p (g t) -> p g t", t=TC)
        t_rk = ctx.gdma(ab.ch_k[par], ab.kbig[par][:, :].rearrange("p (g t) -> p g t", t=SUB),
                        kv3[:, :, s0:s0 + SUB], wait=[ab.kbig_war[par], t_kv])
        t_rv = None
        for g in range(16):
            t_rv = ctx.gdma(ab.ch_v, ab.vflat[g:g + 1, :],
                            ab.KV[64:128, g * TC + s0:g * TC + s0 + SUB],
                            wait=[ab.vflat_war, t_kv] if g == 0 else ())
        return t_rq, t_rk, t_rv

    toks = rebuild(0)
    for sub in range(nsub):
        par = sub % 2
        s0 = sub * SUB
        t_rq, t_rk, t_rv = toks
        BW = TC
        nbg = SUB * 16 // BW
        tpb = BW // 16
        att_banks = pp.acquire_n(nbg)
        ctx.Te.wait_all([w for (_, _, w) in att_banks] + [t_rq, t_rk])
        qk_toks = []
        for bg in range(nbg):
            _, ps, _ = att_banks[bg]
            for tt in range(tpb):
                tok_i = bg * tpb + tt
                f = (lambda eng, ps=ps, ab=ab, tt=tt, tok_i=tok_i, SUB=SUB, par=par:
                     eng.matmul(ps[0:16, tt * 16:(tt + 1) * 16],
                                ab.kbig[par][:, tok_i::SUB], ab.qbig[par][:, tok_i::SUB],
                                start=True, stop=True))
                if tt == tpb - 1:
                    qk_toks.append(ctx.Te.done(f))
                else:
                    ctx.Te.raw(f)
        ab.qbig_war[par] = qk_toks[-1]
        ab.kbig_war[par] = qk_toks[-1]
        e_toks = []
        for bg in range(nbg):
            _, ps, _ = att_banks[bg]
            ctx.S.wait_all([qk_toks[bg], ab.ebuf_war[par]])
            e_toks.append(ctx.S.done(
                lambda eng, ab=ab, ps=ps, bg=bg, sc=c0125, BW=BW, par=par:
                eng.activation(ab.Ebuf[par][0:16, bg * BW:(bg + 1) * BW], ps[0:16, :],
                               AF.Exp, scale=sc)))
        for bg in range(nbg):
            pp.release(att_banks[bg][0], e_toks[bg])
        zbanks = pp.acquire_n(nbg)
        zm_toks = []
        for bg in range(nbg):
            _, ps_z, war_z = zbanks[bg]
            ctx.Te.wait_all([war_z, e_toks[bg], com.ones_tok])
            zm_toks.append(ctx.Te.done(
                lambda eng, ps=ps_z, com=com, ab=ab, bg=bg, BW=BW, par=par:
                eng.matmul(ps[0:1, :], com.ones16[:, :],
                           ab.Ebuf[par][0:16, bg * BW:(bg + 1) * BW],
                           start=True, stop=True)))
        zr_toks = []
        for bg in range(nbg):
            _, ps_z, _ = zbanks[bg]
            ctx.V.wait_all([zm_toks[bg], ab.zr_war if bg == 0 else None])
            zr_toks.append(ctx.V.done(
                lambda eng, ab=ab, ps=ps_z, bg=bg, BW=BW:
                eng.reciprocal(ab.Zr[0:1, bg * BW:(bg + 1) * BW], ps[0:1, :])))
        for bg in range(nbg):
            pp.release(zbanks[bg][0], zr_toks[bg])
        bbanks = pp.acquire_n(nbg)
        zb_toks = []
        for bg in range(nbg):
            _, ps_b, war_b = bbanks[bg]
            ctx.Te.wait_all([war_b, zr_toks[bg]])
            zb_toks.append(ctx.Te.done(
                lambda eng, ps=ps_b, com=com, ab=ab, bg=bg, BW=BW:
                eng.matmul(ps[0:16, :], com.onesrow[0:1, 0:16],
                           ab.Zr[0:1, bg * BW:(bg + 1) * BW],
                           start=True, stop=True)))
        nrm_toks = []
        for bg in range(nbg):
            _, ps_b, _ = bbanks[bg]
            ctx.V.wait(zb_toks[bg])
            t_nm = ctx.V.done(lambda eng, ab=ab, ps=ps_b, bg=bg, BW=BW, par=par:
                              eng.tensor_mul(ab.Ebuf[par][0:16, bg * BW:(bg + 1) * BW],
                                             ab.Ebuf[par][0:16, bg * BW:(bg + 1) * BW],
                                             ps[0:16, :]))
            pp.release(bbanks[bg][0], t_nm)
            nrm_toks.append(t_nm)
        ab.zr_war = zb_toks[-1]
        av_banks = pp.acquire_n(nbg)
        ctx.Te.wait_all([w for (_, _, w) in av_banks] + [t_rv] + nrm_toks)
        av_toks = []
        for bg in range(nbg):
            _, ps, _ = av_banks[bg]
            for tt in range(tpb):
                tok_i = bg * tpb + tt
                f = (lambda eng, ps=ps, ab=ab, tt=tt, tok_i=tok_i, SUB=SUB, par=par:
                     eng.matmul(ps[0:64, tt * 16:(tt + 1) * 16],
                                ab.vflat[:, tok_i::SUB],
                                ab.Ebuf[par][0:16, tok_i * 16:(tok_i + 1) * 16],
                                start=True, stop=True))
                if tt == tpb - 1:
                    av_toks.append(ctx.Te.done(f))
                else:
                    ctx.Te.raw(f)
        ab.ebuf_war[par] = av_toks[-1]
        ab.vflat_war = av_toks[-1]
        if sub + 1 < nsub:
            toks = rebuild(sub + 1)
        ev_toks = []
        for bg in range(nbg):
            _, ps, _ = av_banks[bg]
            ctx.V.wait_all([av_toks[bg], ab.aob_war])
            dst = ab.AOB[:, :].rearrange("p (h t) -> p h t", t=TC)[:, :, s0 + bg * tpb:s0 + (bg + 1) * tpb]
            srcv = ps[0:64, 0:BW].rearrange("p (t h) -> p h t", h=16)
            ev_toks.append(ctx.V.done(lambda eng, dst=dst, srcv=srcv:
                                      eng.tensor_copy(dst, srcv)))
        for bg in range(nbg):
            pp.release(av_banks[bg][0], ev_toks[bg])
        aob_tok = ev_toks[-1]
    ab.aob_war = aob_tok

    rhs = [ab.AOB[:, h * TC:(h + 1) * TC] for h in range(16)]

    def evac_o(bank_list, mm_tok):
        ctx.V.wait(mm_tok)
        last = None
        for (bi, ps, mi) in bank_list:
            last = ctx.V.done(lambda eng, ps=ps, mi=mi, R=R, cs=cs:
                              eng.scalar_tensor_tensor(R[mi][:, cs], ps[:, :], 0.0,
                                                       R[mi][:, cs], ALU.bypass, ALU.add))
        return last
    t_o = emit_linear_krows(ctx, pp, ws64, woutd, D, D, rhs, aob_tok, evac_o,
                            krows=64,
                            bias_rank1=(lambda mi: boutrow[0:1, mi * 128:mi * 128 + 128],
                                        com.onesrowb, com.ones_tok))
    ab.aob_war = t_o
    return t_o


def build_l0a(T=2048, TC=512, SUB=128):
    nc = bass.Bass("TRN2", target_bir_lowering=False, debug=False)
    xin = nc.dram_tensor("xT", [D, T], F32, kind="ExternalInput").ap()
    wq = nc.dram_tensor("Wq", [D, D], F32, kind="ExternalInput").ap()
    wdn = nc.dram_tensor("Wdown", [D, 256], F32, kind="ExternalInput").ap()
    wupd = nc.dram_tensor("Wup", [256, 2048], F32, kind="ExternalInput").ap()
    wo = nc.dram_tensor("Wout", [D, D], F32, kind="ExternalInput").ap()
    dbs = {}
    for nm, n in (("bq", D), ("bdown", 256), ("bup", 2048), ("bout", D)):
        dbs[nm] = nc.dram_tensor(nm, [n], F32, kind="ExternalInput").ap()
    xout = nc.dram_tensor("yT", [D, T], F32, kind="ExternalOutput").ap()
    ctx = Ctx(nc, T, TC, SUB=SUB)
    with ctx.es:
        banks = [ctx.psum("bank", [128, TC]) for _ in range(8)]
        pp = PsumPool(banks)
        com = Common(ctx)
        com.c(1e-5); com.c(-0.5); com.c(0.125, p=16)
        com.finish_setup()
        ln = LNState(ctx)
        ab = AttnBufs(ctx)
        ws = KRowStream(ctx, "ws", D)
        ws64 = KRowStream(ctx, "ws64", D, krows=64)
        wdown = WResident(ctx, wdn, D, 256, name="wdn")
        wup = WResident(ctx, wupd, 256, 2048, name="wup")
        bq, _ = load_bias_cols(ctx, pp, com, dbs["bq"], D, name="bq")
        bdn, _ = load_bias_cols(ctx, pp, com, dbs["bdown"], 256, name="bdn")
        bup, _ = load_bias_cols(ctx, pp, com, dbs["bup"], 2048, name="bup")
        boutrow, _ = load_bias_row(ctx, com, dbs["bout"], D, name="bout", dtype=BF16)
        R = [ctx.sbuf("R", [128, T], F32) for _ in range(KT)]
        XN = ctx.sbuf("XN", [128, KT * TC], BF16)
        xn = [XN[:, i * TC:(i + 1) * TC] for i in range(KT)]
        ch_r = ctx.channel("rload")
        r_toks = [ctx.hdma(ch_r, R[i][:], xin[i * 128:(i + 1) * 128, :]) for i in range(KT)]
        r_tok = r_toks[-1]
        ctx.barrier()
        for c in range(ctx.NCH):
            t_ln = emit_ln_chunk(ctx, pp, com, ln, R, r_tok, c, xn)
            r_tok = emit_l0a_chunk(ctx, pp, com, ab, ws, ws64, xn, t_ln, R, c,
                                   wq, wdown, wup, wo, bq, bdn, bup, boutrow)
        ctx.barrier()
        ch_o = ctx.channel("store")
        st = [ctx.hdma(ch_o, xout[i * 128:(i + 1) * 128, :], R[i][:]) for i in range(KT)]
        ctx.Y.wait(st[-1])
        with nc.Block() as block:
            ctx.build(block)
    return nc



class MoeBufs:
    def __init__(self, ctx, E=8):
        T, TC = ctx.T, ctx.TC
        self.E = E
        self.Lrow = ctx.sbuf("Lrow", [E, T], BF16)
        self.SCT = ctx.sbuf("SCT", [128, (T // 128) * E], F32)
        self.MKT = ctx.sbuf("MKT", [128, (T // 128) * E], F32)
        self.SET = ctx.sbuf("SET", [128, (T // 128) * E], F32)
        self.SErow = ctx.sbuf("SErow", [E, T], BF16)
        self.Mrow = ctx.sbuf("Mrow", [E, T], BF16)
        if getattr(ctx, "_scr_on", False):
            self.se0 = ctx.scr(BF16, 1, TC, base=0, share="sem0", share_elems=TC)
            self.m0 = ctx.scr(BF16, 1, TC, base=32, share="sem0")
            self.b2r = ctx.scr(BF16, 1, D, base=0, share="ber", share_elems=D)
            self.b3r = ctx.scr(BF16, 1, D, base=32, share="ber")
        else:
            self.se0 = ctx.sbuf("se0", [1, TC], BF16)
            self.m0 = ctx.sbuf("m0", [1, TC], BF16)
            self.b2r = ctx.sbuf("b2r", [1, D], BF16)
            self.b3r = ctx.sbuf("b3r", [1, D], BF16)
        self.mx8 = ctx.sbuf("mx8", [128, 8], F32)
        self.rz = ctx.sbuf("rz", [128, T // 128], F32)
        self.SEB = ctx.sbuf("SEB", [128, TC], BF16)
        self.lsc = ctx.sbuf("lsc", [E, 128], F32)
        self.X3 = ctx.sbuf("X3", [128, KT * TC], BF16)
        self.tmp = [ctx.sbuf("mtmp", [128, TC], BF16)]
        self.tmp_war = [None]
        self.ch_se = ctx.channel("chse")
        self.ex_war = None
        self.xnl_war = None
        self.sem_war = None
        self.seb_war = None
        self.xs_war = None
        self.h1_war = None
        self.x3_war = None


def emit_routing_logits(ctx, pp, com, mb, gw, xn, xn_tok, c):
    TC = ctx.TC
    E = mb.E

    def evac_lg(bi, ps, mi, mw, mm_tok):
        ctx.S.wait(mm_tok)
        return ctx.S.done(lambda eng, mb=mb, ps=ps, c=c, TC=TC, E=E:
                          eng.activation(mb.Lrow[:, c * TC:(c + 1) * TC], ps[0:E, :],
                                         AF.Identity))
    return emit_linear_resident(ctx, pp, gw, xn, xn_tok, evac_lg)


def emit_routing_finish(ctx, pp, com, mb, t_lg):
    T, TC = ctx.T, ctx.TC
    E = mb.E
    nt = T // 128
    t_tr = None
    t_ls = None
    for tt in range(nt):
        ctx.V.wait_all([t_lg, t_ls])
        t_cp = ctx.V.done(lambda eng, mb=mb, tt=tt:
                          eng.tensor_copy(mb.lsc[:], mb.Lrow[:, tt * 128:(tt + 1) * 128]))
        bi, ps, war = pp.acquire()
        ctx.Te.wait_all([war, t_cp, com.idf_tok])
        t_mm = ctx.Te.done(lambda eng, ps=ps, mb=mb, com=com, E=E:
                           eng.transpose(ps[0:128, 0:E], mb.lsc[:], com.idf[0:E, 0:E]))
        t_ls = t_mm
        ctx.V.wait(t_mm)
        t_tr = ctx.V.done(lambda eng, mb=mb, ps=ps, tt=tt, E=E:
                          eng.tensor_copy(mb.SCT[:, tt * E:(tt + 1) * E], ps[0:128, 0:E]))
        pp.release(bi, t_tr)
    ctx.S.wait(t_tr)
    t_exp = ctx.S.done(lambda eng, mb=mb: eng.activation(mb.SCT[:], mb.SCT[:], AF.Exp))
    ctx.V.wait(t_exp)
    t_z = ctx.V.done(lambda eng, mb=mb, nt=nt, E=E:
                     eng.reduce_sum(mb.rz[:], mb.SCT[:].rearrange("p (a b) -> p a b", b=E),
                                    axis=AX.X))
    ctx.V.wait(t_z)
    t_rz = ctx.V.done(lambda eng, mb=mb: eng.reciprocal(mb.rz[:], mb.rz[:]))
    last = None
    for tt in range(nt):
        ctx.V.wait(t_rz)
        t_n = ctx.V.done(lambda eng, mb=mb, tt=tt, E=E:
                         eng.tensor_scalar_mul(mb.SCT[:, tt * E:(tt + 1) * E],
                                               mb.SCT[:, tt * E:(tt + 1) * E],
                                               mb.rz[:, tt:tt + 1]))
        ctx.V.wait(t_n)
        t_m8 = ctx.V.done(lambda eng, mb=mb, tt=tt, E=E:
                          eng.max(mb.mx8[:], mb.SCT[:, tt * E:(tt + 1) * E]))
        ctx.V.wait(t_m8)
        t_mk = ctx.V.done(lambda eng, mb=mb, tt=tt, E=E:
                          eng.tensor_scalar(mb.MKT[:, tt * E:(tt + 1) * E],
                                            mb.SCT[:, tt * E:(tt + 1) * E],
                                            mb.mx8[:, 1:2], None, ALU.is_ge))
        ctx.V.wait(t_mk)
        last = ctx.V.done(lambda eng, mb=mb, tt=tt, E=E:
                          eng.tensor_mul(mb.SET[:, tt * E:(tt + 1) * E],
                                         mb.SCT[:, tt * E:(tt + 1) * E],
                                         mb.MKT[:, tt * E:(tt + 1) * E]))
    t_se = t_mrow = None
    for tt in range(nt):
        for (srcbuf, dstbuf) in ((mb.SET, mb.SErow), (mb.MKT, mb.Mrow)):
            bi, ps, war = pp.acquire()
            ctx.Te.wait_all([war, last, com.idf_tok])
            t_mm = ctx.Te.done(lambda eng, ps=ps, srcbuf=srcbuf, tt=tt, com=com, E=E:
                               eng.transpose(ps[0:E, 0:128], srcbuf[:, tt * E:(tt + 1) * E],
                                             com.idf[:, 0:128]))
            ctx.V.wait(t_mm)
            t_ev = ctx.V.done(lambda eng, dstbuf=dstbuf, ps=ps, tt=tt, E=E:
                              eng.tensor_copy(dstbuf[:, tt * 128:(tt + 1) * 128], ps[0:E, 0:128]))
            pp.release(bi, t_ev)
            if dstbuf is mb.SErow:
                t_se = t_ev
            else:
                t_mrow = t_ev
    return (t_se, t_mrow)


def emit_expert_chunk(ctx, pp, com, mb, w1, w3, ws, w2d, e, c, xnv, xn_tok,
                      b1c, se_tok, R, h1buf, xsbuf, b23=None):
    T, TC = ctx.T, ctx.TC
    cs = slice(c * TC, (c + 1) * TC)
    E = mb.E
    t_se, t_mrow = se_tok
    ctx.gdma(mb.ch_se, mb.se0[:], mb.SErow[e:e + 1, cs], wait=[t_se, t_mrow, mb.sem_war])
    t_m0 = ctx.gdma(mb.ch_se, mb.m0[:], mb.Mrow[e:e + 1, cs])
    if b23 is not None:
        b2d_row, b3d_row, ch_br = b23
        ctx.gdma(ch_br, mb.b2r[:], b2d_row, wait=[mb.h1_war])
        mb.b2r_tok = ctx.gdma(ch_br, mb.b3r[:], b3d_row)
    bi, ps, war = pp.acquire()
    ctx.Te.wait_all([war, t_m0, com.ones_tok])
    t_mm = ctx.Te.done(lambda eng, ps=ps, com=com, mb=mb, TC=TC:
                       eng.matmul(ps[:, :], com.onesrowb[0:1, 0:128],
                                  mb.se0[0:1, :], start=True, stop=True))
    ctx.S.wait_all([t_mm, mb.seb_war])
    t_seb = ctx.S.done(lambda eng, mb=mb, ps=ps: eng.activation(mb.SEB[:], ps[:, :], AF.Copy))
    pp.release(bi, t_seb)
    xn_toks = list(xn_tok) if isinstance(xn_tok, tuple) else [xn_tok]
    xs_tok = None
    for i in range(KT):
        ctx.V.wait_all([t_seb, mb.xs_war] + xn_toks)
        xs_tok = ctx.V.done(lambda eng, mb=mb, xsbuf=xsbuf, xnv=xnv, i=i, TC=TC:
                            eng.tensor_mul(xsbuf[:, i * TC:(i + 1) * TC],
                                           xnv[i], mb.SEB[:]))
    mb.xnl_war = xs_tok
    mb.seb_war = xs_tok
    xs = [xsbuf[:, i * TC:(i + 1) * TC] for i in range(KT)]
    def evac_g1(bi, ps, mi, mw, mm_tok):
        ctx.S.wait_all([mm_tok, mb.h1_war])
        return ctx.S.done(lambda eng, h1buf=h1buf, ps=ps, mi=mi, b=b1c, e=e, TC=TC:
                          eng.activation(h1buf[:, mi * TC:(mi + 1) * TC], ps[:, :],
                                         AF.Silu, bias=b[:, mi:mi + 1]))
    t_h1 = emit_linear_resident(ctx, pp, w1, xs, xs_tok, evac_g1)
    def evac_x3(bi, ps, mi, mw, mm_tok):
        ctx.S.wait_all([mm_tok, mb.x3_war])
        return ctx.S.done(lambda eng, mb=mb, ps=ps, mi=mi, TC=TC:
                          eng.activation(mb.X3[:, mi * TC:(mi + 1) * TC], ps[:, :], AF.Copy))
    ctx.Te.wait(mb.b2r_tok)
    t_x3 = emit_linear_resident(
        ctx, pp, w3, xs, xs_tok, evac_x3,
        extra_k=[(lambda mi: mb.b3r[0:1, mi * 128:(mi + 1) * 128],
                  mb.m0[0:1, :], t_m0)])
    mb.w13_reader = ctx.Te.tok()
    mb.sem_war = t_x3
    mb.xs_war = t_x3
    rhs = [h1buf[:, kh * TC:(kh + 1) * TC] for kh in range(KT)]

    def evac_o(bank_list, mm_tok):
        mb.h1_war = mm_tok
        ctx.V.wait_all([mm_tok, t_x3])
        last = None
        t_p = None
        for (bi, ps, mi) in bank_list:
            ts_ = mi % len(mb.tmp)
            ctx.V.wait(mb.tmp_war[ts_])
            t_p = ctx.V.done(lambda eng, mb=mb, ts_=ts_, ps=ps, mi=mi, TC=TC:
                             eng.tensor_mul(mb.tmp[ts_][:], ps[:, :],
                                            mb.X3[:, mi * TC:(mi + 1) * TC]))
            ctx.V.wait_all([t_p, mb.r_war])
            last = ctx.V.done(lambda eng, mb=mb, ts_=ts_, mi=mi, R=R, cs=cs:
                              eng.tensor_add(R[mi][:, cs], R[mi][:, cs], mb.tmp[ts_][:]))
            mb.tmp_war[ts_] = last
            mb.r_war = last
        mb.x3_war = t_p
        return last
    tok = emit_linear_krows(
        ctx, pp, ws, w2d, D, D, rhs, t_h1, evac_o,
        bias_rank1=(lambda mi: mb.b2r[0:1, mi * 128:(mi + 1) * 128],
                    com.onesrowb, mb.b2r_tok))
    mb.ex_war = tok
    return tok


def build_l1f(T=2048, TC=512, E=8, mgw=128):
    H = D
    I = 2048
    nc = bass.Bass("TRN2", target_bir_lowering=False, debug=False)
    xin = nc.dram_tensor("xT", [D, T], F32, kind="ExternalInput").ap()
    gwd = nc.dram_tensor("gwT", [D, E], F32, kind="ExternalInput").ap()
    w1d = nc.dram_tensor("w1T", [E, D, H], F32, kind="ExternalInput").ap()
    w2d = nc.dram_tensor("w2T", [E, H, D], F32, kind="ExternalInput").ap()
    w3d = nc.dram_tensor("w3T", [E, D, H], F32, kind="ExternalInput").ap()
    b1d = nc.dram_tensor("b1", [E, H], F32, kind="ExternalInput").ap()
    b2d = nc.dram_tensor("b2", [E, D], F32, kind="ExternalInput").ap()
    b3d = nc.dram_tensor("b3", [E, D], F32, kind="ExternalInput").ap()
    sw1d = nc.dram_tensor("sw1T", [D, I], F32, kind="ExternalInput").ap()
    sw2d = nc.dram_tensor("sw2T", [D, I], F32, kind="ExternalInput").ap()
    sw3d = nc.dram_tensor("sw3T", [I, D], F32, kind="ExternalInput").ap()
    xout = nc.dram_tensor("yT", [D, T], F32, kind="ExternalOutput").ap()
    ctx = Ctx(nc, T, TC)
    with ctx.es:
        banks = [ctx.psum("bank", [128, TC]) for _ in range(8)]
        pp = PsumPool(banks)
        com = Common(ctx)
        com.c(1e-5); com.c(-0.5); com.c(1.0)
        com.finish_setup()
        ln = LNState(ctx)
        mb = MoeBufs(ctx, E=E)
        mb.ex_war = None
        mb.r_war = None
        sw = SwigluBufs(ctx, I, mgw)
        gw = WResident(ctx, gwd, D, E, name="gw")
        b1cs = []
        for e in range(E):
            b1c, _ = load_bias_cols(ctx, pp, com,
                                    b1d[e:e + 1, :].rearrange("o (a b) -> (o a) b", b=128),
                                    H, name=f"b1c{e}", shaped=True)
            b1cs.append(b1c)

        R = [ctx.sbuf("R", [128, T], F32) for _ in range(KT)]
        XNF = ctx.sbuf("XNF", [128, KT * T], BF16)
        xn_chunks = [[XNF[:, i * T + c * TC:i * T + (c + 1) * TC] for i in range(KT)]
                     for c in range(ctx.NCH)]
        ch_r = ctx.channel("rload")
        r_toks = [ctx.hdma(ch_r, R[i][:], xin[i * 128:(i + 1) * 128, :]) for i in range(KT)]
        r_tok = r_toks[-1]
        ctx.barrier()
        xn_toks = []
        t_lg = None
        for c in range(ctx.NCH):
            xn_toks.append(emit_ln_chunk(ctx, pp, com, ln, R, r_tok, c, xn_chunks[c]))
            t_lg = emit_routing_logits(ctx, pp, com, mb, gw, xn_chunks[c], xn_toks[c], c)
        se_tok = emit_routing_finish(ctx, pp, com, mb, t_lg)
        for c in range(ctx.NCH):
            t_sh = emit_swiglu_chunk(ctx, pp, com, sw, xn_chunks[c], xn_toks[c],
                                     sw1d, sw2d, sw3d, None, None, R, c)
            mb.r_war = t_sh
        h1buf = sw.hbuf[:, 0:KT * TC]
        xsbuf = sw.hbuf[:, KT * TC:2 * KT * TC]
        w1sb = ctx.sbuf("w1sb", [128, KT * H], BF16)
        w3sb = ctx.sbuf("w3sb", [128, KT * H], BF16)
        ch_w1 = ctx.channel("chw1")
        ch_w3 = ctx.channel("chw3")
        ws2 = sw.w3s
        wtok = None
        mb.ex_war = mb.r_war
        mb.xs_war = mb.r_war
        mb.h1_war = mb.r_war
        ch_br = ctx.channel("chbr")
        for e in range(E):
            w1 = WResident(ctx, w1d[e], D, H, name=f"w1_{e}", sbuf=w1sb, ch=ch_w1, war=wtok)
            w3 = WResident(ctx, w3d[e], D, H, name=f"w3_{e}", sbuf=w3sb, ch=ch_w3, war=wtok)
            ctx.gdma(ch_br, mb.b2r[:], b2d[e:e + 1, :], wait=[wtok])
            mb.b2r_tok = ctx.gdma(ch_br, mb.b3r[:], b3d[e:e + 1, :])
            for c in range(ctx.NCH):
                xnv_c = [XNF[:, i * T + c * TC:i * T + (c + 1) * TC] for i in range(KT)]
                wtok = emit_expert_chunk(ctx, pp, com, mb, w1, w3, ws2, w2d[e], e, c,
                                         xnv_c, xn_toks[c], b1cs[e], se_tok,
                                         R, h1buf, xsbuf)
        ctx.barrier()
        ch_o = ctx.channel("store")
        st = [ctx.hdma(ch_o, xout[i * 128:(i + 1) * 128, :], R[i][:]) for i in range(KT)]
        ctx.Y.wait(st[-1])
        with nc.Block() as block:
            ctx.build(block)
    return nc



def build_fused(T=2048, TC=512, SUB=128, PB=40000, PF=6144, blocks=("l0a", "l0f", "l1a", "l1f")):
    E, HF, IS, DL = 8, 3072, 2048, 256
    nc = bass.Bass("TRN2", target_bir_lowering=False, debug=False)
    d = {}
    d["xT"] = nc.dram_tensor("xT", [D, T], F32, kind="ExternalInput").ap()
    for nm, shape, dt in (
        ("Wq", [D, D], F32), ("Wdown", [D, DL], F32), ("Wup", [DL, 2 * D], F32),
        ("Wout0", [D, D], F32), ("bq", [D], F32), ("bdown", [DL], F32),
        ("bup", [2 * D], F32), ("bout0", [D], F32),
        ("wgT", [D, HF], I8), ("wuT", [D, HF], I8), ("wdT", [HF, D], I8),
        ("Wz", [D, D], F32), ("Wx", [D, D], F32), ("Wdt", [D, D], F32),
        ("Wout1", [D, D], F32), ("WB", [D, 16], F32), ("WC", [D, 16], F32),
        ("bz", [D], F32), ("bx", [D], F32), ("bdt", [D], F32), ("bout1", [D], F32),
        ("bB", [16], F32), ("bC", [16], F32), ("A_log", [D, 16], F32),
        ("gwT", [D, E], F32), ("w1T", [E, D, D], F32), ("w2T", [E, D, D], F32),
        ("w3T", [E, D, D], F32), ("b1", [E, D], F32), ("b2", [E, D], F32),
        ("b3", [E, D], F32), ("sw1T", [D, IS], F32), ("sw2T", [D, IS], F32),
        ("sw3T", [IS, D], F32), ("Wro_hi", [128, T], F32), ("Wro_lo", [128, T], F32),
    ):
        d[nm] = nc.dram_tensor(nm, shape, dt, kind="ExternalInput").ap()
    yout = nc.dram_tensor("yT", [D, T], F32, kind="ExternalOutput").ap()

    ctx = Ctx(nc, T, TC, SUB=SUB)
    with ctx.es:
        banks = [ctx.psum("bank", [128, TC]) for _ in range(8)]
        pp = PsumPool(banks)
        com = Common(ctx)
        com.c(1e-5); com.c(-0.5); com.c(1.0); com.c(0.125, p=16)
        r127 = com.crecip(127.0)
        com.finish_setup()
        ln = LNState(ctx)
        ws = KRowStream(ctx, "ws", D)
        ws64 = KRowStream(ctx, "ws64", D, krows=64)
        bq, _ = load_bias_cols(ctx, pp, com, d["bq"], D, name="bq")
        bdn, _ = load_bias_cols(ctx, pp, com, d["bdown"], DL, name="bdn")
        bup, _ = load_bias_cols(ctx, pp, com, d["bup"], 2 * D, name="bup")
        bout0, _ = load_bias_row(ctx, com, d["bout0"], D, name="bout0", dtype=BF16)
        bz, _ = load_bias_cols(ctx, pp, com, d["bz"], D, name="bz")
        bx, _ = load_bias_cols(ctx, pp, com, d["bx"], D, name="bx")
        bdt, _ = load_bias_cols(ctx, pp, com, d["bdt"], D, name="bdt")
        bout1, _ = load_bias_row(ctx, com, d["bout1"], D, name="bout1", dtype=BF16)
        bB16 = ctx.sbuf("bB16", [16, 1], F32)
        bC16 = ctx.sbuf("bC16", [16, 1], F32)
        ctx.hdma(com.setup_ch, bB16[:], d["bB"].rearrange("(a b) -> a b", b=1))
        ctx.hdma(com.setup_ch, bC16[:], d["bC"].rearrange("(a b) -> a b", b=1))
        b1cs = []
        for e in range(E):
            b1c, _ = load_bias_cols(ctx, pp, com,
                                    d["b1"][e:e + 1, :].rearrange("o (a b) -> (o a) b", b=128),
                                    D, name=f"b1c{e}", shaped=True)
            b1cs.append(b1c)
        R = [ctx.sbuf("R", [128, T], F32) for _ in range(KT)]
        XN = ctx.sbuf("XN", [128, KT * TC], BF16)
        xn = [XN[:, i * TC:(i + 1) * TC] for i in range(KT)]
        ctx.scratch_init(PB, PF)
        ch_r = ctx.channel("rload")
        r_toks = [ctx.hdma(ch_r, R[i][:], d["xT"][i * 128:(i + 1) * 128, :]) for i in range(KT)]
        r_tok = r_toks[-1]
        ctx.barrier()

        W1B = nc.dram_tensor("w1b", [E, D, D], BF16, kind="Internal").ap()
        W2B = nc.dram_tensor("w2b", [E, D, D], BF16, kind="Internal").ap()
        W3B = nc.dram_tensor("w3b", [E, D, D], BF16, kind="Internal").ap()
        SW1B = nc.dram_tensor("sw1b", [D, IS], BF16, kind="Internal").ap()
        SW2B = nc.dram_tensor("sw2b", [D, IS], BF16, kind="Internal").ap()
        SW3B = nc.dram_tensor("sw3b", [IS, D], BF16, kind="Internal").ap()
        ch_bn = [ctx.channel(f"bounce{i}") for i in range(8)]
        _bn = [0]
        _bnq = [(SW1B[:, :], d["sw1T"]), (SW2B[:, :], d["sw2T"]), (SW3B[:, :], d["sw3T"])]
        for e in range(E):
            _bnq += [(W1B[e], d["w1T"][e]), (W2B[e], d["w2T"][e]), (W3B[e], d["w3T"][e])]

        def pump_bounce(k):
            while k > 0 and _bnq:
                dst, srcd = _bnq.pop(0)
                ctx.gdma(ch_bn[_bn[0] % 8], dst, srcd)
                _bn[0] += 1
                k -= 1

        def emit_bounce():
            pump_bounce(len(_bnq))

        if "l0a" in blocks:
            ctx.scratch_begin()
            ab = AttnBufs(ctx)
            wdown = WResident(ctx, d["Wdown"], D, DL, name="wdn")
            wup = WResident(ctx, d["Wup"], DL, 2 * D, name="wup")
            wro = ctx.sbuf("wro", [128, T], F32)
            wrol = ctx.sbuf("wrol", [128, T], F32)
            ctx.scratch_end()
            ch_w = ctx.channel("wroload")
            ctx.hdma(ch_w, wro[0:128, :], d["Wro_hi"][:])
            t_wl = ctx.hdma(ch_w, wrol[0:128, :], d["Wro_lo"][:])
            ctx.V.wait(t_wl)
            wro_tok = ctx.V.done(lambda eng, a=wro, b=wrol:
                                 eng.tensor_add(a[0:128, :], a[0:128, :], b[0:128, :]))
            ab.wro = wro
            ab.wro_tok = wro_tok
            for c in range(ctx.NCH):
                t_ln = emit_ln_chunk(ctx, pp, com, ln, R, r_tok, c, xn)
                r_tok = emit_l0a_chunk(ctx, pp, com, ab, ws, ws64, xn, t_ln, R, c,
                                       d["Wq"], wdown, wup, d["Wout0"], bq, bdn, bup, bout0)
                if "l1f" in blocks:
                    pump_bounce(3)
            ctx.barrier()

        if "l0f" in blocks:
            ctx.scratch_begin()
            sw = SwigluBufs(ctx, HF, 256, w3s=ws)
            ctx.scratch_end()
            for c in range(ctx.NCH):
                t_ln = emit_ln_chunk(ctx, pp, com, ln, R, r_tok, c, xn)
                r_tok = emit_swiglu_chunk(ctx, pp, com, sw, xn, t_ln,
                                          d["wgT"], d["wuT"], d["wdT"], r127, r127, R, c)
                if "l1f" in blocks:
                    pump_bounce(4)
            ctx.barrier()

        if "l1a" in blocks:
            ctx.scratch_begin()
            sb = SSMBufs(ctx)
            wB = WResident(ctx, d["WB"], D, 16, name="wB")
            wC = WResident(ctx, d["WC"], D, 16, name="wC")
            Araw = ctx.sbuf("Araw", [128, 128], F32)
            Asb = Araw
            ctx.scratch_end()
            ch_a = ctx.channel("aload")
            for ii in range(KT):
                ta = ctx.hdma(ch_a, Araw[:, ii * 16:(ii + 1) * 16],
                              d["A_log"][ii * 128:(ii + 1) * 128, :])
            ctx.S.wait(ta)
            tae = ctx.S.done(lambda eng: eng.activation(Asb[:], Araw[:], AF.Exp))
            ctx.V.wait(tae)
            A_tok = ctx.V.done(lambda eng: eng.tensor_scalar_mul(Asb[:], Asb[:], -1.0))
            if "l1f" in blocks:
                emit_bounce()
            r0_tok = r_tok
            t_lns = {0: emit_ln_chunk(ctx, pp, com, ln, R, r0_tok, 0, xn)}

            def mk_post(cc):
                if cc + 1 >= ctx.NCH:
                    return None

                def post(t_C):
                    t_lns[cc + 1] = emit_ln_chunk(ctx, pp, com, ln, R, r0_tok,
                                                  cc + 1, xn, xn_war=t_C)
                return post
            for c in range(ctx.NCH):
                r_tok = emit_l1a_chunk(ctx, pp, com, ln, sb, ws, xn, t_lns.pop(c), R, c,
                                       d["Wz"], d["Wx"], d["Wdt"], d["Wout1"],
                                       wB, wC, bz, bx, bdt, bB16, bC16, bout1,
                                       Asb, A_tok, first=(c == 0), post_mm=mk_post(c))
            ctx.barrier()

        if "l1f" in blocks:
            if "l1a" not in blocks:
                emit_bounce()
                ctx.barrier()
            ctx.scratch_begin()
            mb = MoeBufs(ctx, E=E)
            mb.r_war = None
            sw2 = SwigluBufs(ctx, IS, 128, w3s=ws, nslots=1)
            gw = WResident(ctx, d["gwT"], D, E, name="gw")
            xnl = XN
            w1sb = ctx.sbuf("w1sb", [128, KT * D], BF16)
            w3sb = ctx.sbuf("w3sb", [128, KT * D], BF16)
            ctx.scratch_end()
            XNFD = nc.dram_tensor("xnfd", [D, T], BF16, kind="Internal").ap()
            ch_xf = ctx.channel("xnfst")
            ch_xl = ctx.channel("xnfld")
            ch_xl2 = ctx.channel("xnfld2")
            t_lg = None
            xst_tok = None
            for c in range(ctx.NCH):
                t_ln = emit_ln_chunk(ctx, pp, com, ln, R, r_tok, c, xn, xn_war=xst_tok)
                t_lg = emit_routing_logits(ctx, pp, com, mb, gw, xn, t_ln, c)
                t_sh = emit_swiglu_chunk(ctx, pp, com, sw2, xn, t_ln,
                                         SW1B, SW2B, SW3B, None, None, R, c)
                mb.r_war = t_sh
                xst_tok = ctx.hdma(
                    ch_xf,
                    XNFD[:, c * TC:(c + 1) * TC].rearrange("(a p) t -> p a t", p=128),
                    XN[:, :].rearrange("p (a t) -> p a t", a=KT), wait=[t_ln])
                r_tok = t_sh
            se_tok = emit_routing_finish(ctx, pp, com, mb, t_lg)
            h1buf = sw2.hbuf[:, 0:KT * TC]
            xsbuf = sw2.hbuf[:, KT * TC:2 * KT * TC]
            ch_w1 = ctx.channel("chw1")
            ch_w3 = ctx.channel("chw3")
            ch_br = ctx.channel("chbr")
            wtok = None
            mb.ex_war = mb.r_war
            mb.xnl_war = mb.r_war
            mb.sem_war = None
            mb.seb_war = None
            mb.xs_war = mb.r_war
            mb.h1_war = mb.r_war
            mb.x3_war = None
            xnv = [xnl[:, i * TC:(i + 1) * TC] for i in range(KT)]
            for e in range(E):
                w1 = WResident(ctx, d["w1T"][e], D, D, name=f"w1_{e}", sbuf=w1sb, ch=ch_w1, war=wtok)
                w3 = WResident(ctx, d["w3T"][e], D, D, name=f"w3_{e}", sbuf=w3sb, ch=ch_w3, war=wtok)
                ctx.gdma(ch_br, mb.b2r[:], d["b2"][e:e + 1, :], wait=[wtok])
                mb.b2r_tok = ctx.gdma(ch_br, mb.b3r[:], d["b3"][e:e + 1, :])
                for c in range(ctx.NCH):
                    hd = KT // 2
                    xl_a = ctx.hdma(
                        ch_xl, xnl[:, 0:hd * TC].rearrange("p (a t) -> p a t", a=hd),
                        XNFD[0:hd * 128, c * TC:(c + 1) * TC].rearrange(
                            "(a p) t -> p a t", p=128),
                        wait=[mb.xnl_war, xst_tok])
                    xl_b = ctx.hdma(
                        ch_xl2, xnl[:, hd * TC:].rearrange("p (a t) -> p a t", a=KT - hd),
                        XNFD[hd * 128:, c * TC:(c + 1) * TC].rearrange(
                            "(a p) t -> p a t", p=128),
                        wait=[mb.xnl_war, xst_tok])
                    xl_tok = (xl_a, xl_b)
                    wtok = emit_expert_chunk(ctx, pp, com, mb, w1, w3, ws, W2B[e], e, c,
                                             xnv, xl_tok, b1cs[e], se_tok,
                                             R, h1buf, xsbuf)
        ctx.barrier()
        ch_o = ctx.channel("store")
        st = [ctx.hdma(ch_o, yout[i * 128:(i + 1) * 128, :], R[i][:]) for i in range(KT)]
        ctx.Y.wait(st[-1])
        with nc.Block() as block:
            ctx.build(block)
    return ctx, nc



_CACHE = {}


def _ct(a):
    return np.ascontiguousarray(a)


def kernel(L0_attn_x, L0_attn_Wq, L0_attn_bq, L0_attn_Wdown, L0_attn_bdown,
           L0_attn_Wup, L0_attn_bup, L0_attn_Wout, L0_attn_bout,
           L0_ffn_w_gate, L0_ffn_w_up, L0_ffn_w_down,
           L1_attn_Wz, L1_attn_bz, L1_attn_Wx, L1_attn_bx, L1_attn_WB, L1_attn_bB,
           L1_attn_WC, L1_attn_bC, L1_attn_Wdt, L1_attn_bdt, L1_attn_A_log,
           L1_attn_Wout, L1_attn_bout,
           L1_ffn_gate_w, L1_ffn_w1, L1_ffn_b1, L1_ffn_w2, L1_ffn_b2,
           L1_ffn_w3, L1_ffn_b3, L1_ffn_sw1, L1_ffn_sw2, L1_ffn_sw3):
    if "f" not in _CACHE:
        _CACHE["f"] = build_fused()
    _, nc = _CACHE["f"]
    x = np.asarray(L0_attn_x, dtype=np.float32)
    B = x.shape[0]
    T = x.shape[1]
    if "wro" not in _CACHE:
        import jax
        import jax.numpy as jnp
        _cpu = jax.local_devices(backend="cpu")[0]
        with jax.default_device(_cpu):
            Dh = 64
            inv = 1.0 / 10000.0 ** (jnp.arange(0, Dh, 2, dtype=jnp.float32) / Dh)
            tt = jnp.arange(T, dtype=jnp.float32)
            fr = tt[:, None] * inv[None, :]
            _c = np.asarray(jnp.cos(fr), np.float64)
            _s = np.asarray(jnp.sin(fr), np.float64)
        w64 = _c ** 2 + _s ** 2
        whi = w64.astype(np.float32)
        wlo = (w64 - whi.astype(np.float64)).astype(np.float32)
        rows = np.arange(128) % 32
        _CACHE["wro"] = (np.ascontiguousarray(whi.T[rows, :]),
                         np.ascontiguousarray(wlo.T[rows, :]))
    wro_hi, wro_lo = _CACHE["wro"]
    base = {"Wq": _ct(L0_attn_Wq), "Wdown": _ct(L0_attn_Wdown), "Wup": _ct(L0_attn_Wup),
            "Wout0": _ct(L0_attn_Wout), "bq": _ct(L0_attn_bq), "bdown": _ct(L0_attn_bdown),
            "bup": _ct(L0_attn_bup), "bout0": _ct(L0_attn_bout),
            "wgT": _ct(L0_ffn_w_gate.T).astype(np.int8),
            "wuT": _ct(L0_ffn_w_up.T).astype(np.int8),
            "wdT": _ct(L0_ffn_w_down.T).astype(np.int8),
            "Wz": _ct(L1_attn_Wz), "Wx": _ct(L1_attn_Wx), "Wdt": _ct(L1_attn_Wdt),
            "Wout1": _ct(L1_attn_Wout), "WB": _ct(L1_attn_WB), "WC": _ct(L1_attn_WC),
            "bz": _ct(L1_attn_bz), "bx": _ct(L1_attn_bx), "bdt": _ct(L1_attn_bdt),
            "bout1": _ct(L1_attn_bout), "bB": _ct(L1_attn_bB), "bC": _ct(L1_attn_bC),
            "A_log": _ct(L1_attn_A_log), "gwT": _ct(L1_ffn_gate_w.T),
            "w1T": _ct(L1_ffn_w1.transpose(0, 2, 1)),
            "w2T": _ct(L1_ffn_w2.transpose(0, 2, 1)),
            "w3T": _ct(L1_ffn_w3.transpose(0, 2, 1)),
            "b1": _ct(L1_ffn_b1), "b2": _ct(L1_ffn_b2), "b3": _ct(L1_ffn_b3),
            "sw1T": _ct(L1_ffn_sw1.T), "sw2T": _ct(L1_ffn_sw2.T), "sw3T": _ct(L1_ffn_sw3.T),
            "Wro_hi": wro_hi, "Wro_lo": wro_lo}
    in_maps = [dict(base, xT=_ct(x[c].T)) for c in range(B)]
    res = run_bass_kernel_spmd(nc, in_maps, core_ids=list(range(B)))
    out = np.stack([np.asarray(res.results[c]["yT"]).T for c in range(B)])
    if out.dtype == np.float32 or out.dtype == np.float64:
        return out
    return out.astype(np.float32)
